# version 23 (final)
# baseline (speedup 1.0000x reference)
import contextlib
import math
import numpy as np

import concourse.bass as bass
import concourse.mybir as mybir
from concourse.bass_utils import run_bass_kernel_spmd

F32 = mybir.dt.float32
BF16 = mybir.dt.bfloat16
AF = mybir.ActivationFunctionType
ALU = mybir.AluOpType

ENGINES = ("sync", "gpsimd", "tensor", "scalar", "vector")


class Cfg:
    def __init__(self, DM=2048, TT=4096, IN=8192, eps=1e-5, C=2.0, debug=False, debug2=False):
        self.DM, self.TT, self.IN, self.eps, self.C = DM, TT, IN, eps, C
        self.debug = debug
        self.debug2 = debug2
        self.DCH = DM // 128
        self.ICH = IN // 128
        self.NH = DM // 128
        self.NG = self.NH // 4
        self.Q = TT // 4
        self.NJ = self.Q // 512
        self.SCH = TT // 128
        self.NQC = self.Q // 128
        self.NT = TT // 512
        self.att_scale = TT ** -0.5
        self.s1 = 1.0 / (64.0 * math.sqrt(DM))
        self.s2 = 1.0 / (64.0 * math.sqrt(IN))
        self.expC = math.exp(-C)
        assert self.NJ >= 1 and self.NH % 4 == 0 and self.Q % 512 == 0


def mask_kind(cfg, j, c):
    if c < cfg.NQC:
        rel = c - 4 * j
        if rel < 0:
            return "ones"
        if rel < 4:
            return str(rel)
        return "zero"
    return "ones"


class Sem:
    def __init__(self, handle):
        self.h = handle
        self.n = 0


class B:

    def __init__(self, cfg):
        self.cfg = cfg
        self.nc = bass.Bass("TRN2", target_bir_lowering=False,
                            detect_race_conditions=False)
        self.q = {e: [] for e in ENGINES}
        self._last_wait = {}
        self.ctx = contextlib.ExitStack()
        self.ps_last = {}

    def on(self, eng, fn):
        self.q[eng].append(fn)

    def wait(self, eng, sem, val=None):
        if val is None:
            val = sem.n
        if val <= 0:
            return
        key = (eng, id(sem))
        if self._last_wait.get(key, -1) >= val:
            return
        self._last_wait[key] = val
        self.on(eng, lambda e, h=sem.h, v=val: e.wait_ge(h, v))

    def dma(self, queue, dst, src, dsem):
        dsem.n += 16
        self.on(queue, lambda e, a=dst, b=src, h=dsem.h: e.dma_start(out=a, in_=b).then_inc(h, 16))
        return dsem.n

    def step(self, eng, esem, fn):
        esem.n += 1
        self.on(eng, lambda e, f=fn, h=esem.h: f(e).then_inc(h, 1))
        return esem.n

    def quiet(self, eng, fn):
        self.on(eng, lambda e, f=fn: f(e))

    def sbuf(self, name, shape, dtype, stack=None):
        return (stack or self.ctx).enter_context(self.nc.sbuf_tensor(name, shape, dtype))

    def psum(self, name, shape, dtype=F32):
        return self.ctx.enter_context(self.nc.psum_tensor(name, shape, dtype))

    def sem(self, name):
        return Sem(self.ctx.enter_context(self.nc.semaphore(name)))

    def ps_guard(self, eng, banks):
        for bk in banks:
            if bk in self.ps_last:
                s, v = self.ps_last[bk]
                self.wait(eng, s, v)

    def ps_release(self, banks, sem, val):
        for bk in banks:
            self.ps_last[bk] = (sem, val)

    def barrier(self):
        for e in ENGINES:
            for s in self.all_sems:
                self.wait(e, s)

    def build(self):
        cfg = self.cfg
        nc = self.nc
        DM, TT, IN = cfg.DM, cfg.TT, cfg.IN
        DCH, ICH, NG, Q, NJ = cfg.DCH, cfg.ICH, cfg.NG, cfg.Q, cfg.NJ
        SCH, NT = cfg.SCH, cfg.NT

        dt = nc.dram_tensor
        self.xt = dt("xt", [DM, TT], F32, kind="ExternalInput").ap()
        self.wq = dt("wq", [DM, DM], F32, kind="ExternalInput").ap()
        self.wk = dt("wk", [DM, DM], F32, kind="ExternalInput").ap()
        self.wv = dt("wv", [DM, DM], F32, kind="ExternalInput").ap()
        self.wo = dt("wo", [DCH, DCH, 128, 128], F32, kind="ExternalInput").ap()
        self.wug = dt("wug", [ICH, 2, DCH, 128, 128], F32, kind="ExternalInput").ap()
        self.wdn = dt("wdn", [DCH, ICH, 128, 128], F32, kind="ExternalInput").ap()
        self.suug = dt("suug", [128, DM // 128], F32, kind="ExternalInput").ap()
        self.kvalid = dt("kvalid", [1, TT], F32, kind="ExternalInput").ap()
        self.upsc_in = dt("upsc_in", [128, ICH], F32, kind="ExternalInput").ap()
        self.gtsc_in = dt("gtsc_in", [128, ICH], F32, kind="ExternalInput").ap()
        self.dnsc_in = dt("dnsc_in", [128, DCH], F32, kind="ExternalInput").ap()
        self.outy = dt("outy", [DM, Q], F32, kind="ExternalOutput").ap()
        kk = "ExternalOutput" if cfg.debug2 else "Internal"
        self.ao_dram = dt("ao_dram", [DCH, 128, Q], BF16, kind=kk).ap()
        if cfg.debug:
            self.dbg1 = dt("dbg1", [128, ICH * 512], BF16, kind="Internal").ap()
            self.dbg2 = dt("dbg2", [128, ICH * 512], BF16, kind="Internal").ap()
        self.xmid_dram = dt("xmid_dram", [DCH, 128, Q], F32, kind=kk).ap()
        if cfg.debug2:
            self.ao2_out = dt("ao2_out", [DCH, 128, Q], BF16, kind="ExternalOutput").ap()
            self.hsu_out = dt("hsu_out", [128, DCH], F32, kind="ExternalOutput").ap()
            self.x1_out = dt("x1_out", [128, DCH * Q], BF16, kind="ExternalOutput").ap()
            self.hb_out = dt("hb_out", [128, (IN // 128) * 512], BF16, kind="ExternalOutput").ap()

        with self.ctx:
            self.dG = self.sem("dG")
            self.dS = self.sem("dS")
            self.eP = self.sem("eP")
            self.eA = self.sem("eA")
            self.eV = self.sem("eV")
            self.eG = self.sem("eG")
            self.all_sems = [self.dG, self.dS, self.eP, self.eA, self.eV, self.eG]

            self.ones_c = self.sbuf("ones_c", [128, 1], BF16)
            self.ones_cf = self.sbuf("ones_cf", [128, 1], F32)
            self.ones_r = self.sbuf("ones_r", [1, 128], F32)
            self.one11 = self.sbuf("one11", [1, 1], F32)
            self.hmat = self.sbuf("hmat", [128, 128], BF16)
            self.hneg = self.sbuf("hneg", [128, 128], BF16)
            self.upsc = self.sbuf("upsc", [128, ICH], F32)
            self.gtsc = self.sbuf("gtsc", [128, ICH], F32)
            self.dnsc = self.sbuf("dnsc", [128, DCH], F32)
            self.negC = self.sbuf("negC", [128, 1], F32)
            self.epsc = self.sbuf("epsc", [1, 1], F32)

            self.PS_S = self.psum("PS_S", [128, 3072], F32)
            self.PS_AV = self.psum("PS_AV", [128, 1024], F32)

            self.setup()
            self.phase_attention()
            self.barrier()
            if cfg.debug2:
                self.dma("sync", self.ao2_out, self.ao_dram[:, :, :], self.dS)
                self.wait("gpsimd", self.dS)
            self.HB = self.sbuf("HB", [128, (IN // 128) * 512], BF16)
            self.X1 = self.sbuf("X1", [128, DCH * Q], BF16)
            self.phase_wo_ln2()
            self.phase_ffn()
            self.wait("sync", self.dS)
            self.emit()
        return self.nc

    def setup(self):
        cfg = self.cfg
        self.quiet("vector", lambda v: v.memset(self.ones_c[:, :], 1.0))
        self.quiet("vector", lambda v: v.memset(self.ones_cf[:, :], 1.0))
        self.quiet("vector", lambda v: v.memset(self.ones_r[:, :], 1.0))
        self.quiet("vector", lambda v: v.memset(self.one11[:, :], 1.0))
        self.quiet("vector", lambda v: v.memset(self.negC[:, :], -self.cfg.C))
        self.quiet("vector", lambda v: v.memset(self.epsc[:, :], self.cfg.eps))
        self.quiet("gpsimd", lambda g: g.memset(self.hmat[:, :], 0.0))
        self.quiet("gpsimd", lambda g: g.memset(self.hneg[:, :], 0.0))
        self.quiet("gpsimd", lambda g: g.memset(self.hmat[0:1, 0:1], 1.0))
        self.quiet("gpsimd", lambda g: g.memset(self.hneg[0:1, 0:1], -1.0))
        hs = self.dG
        for n in (1, 2, 4, 8, 16, 32, 64):
            self.dma("gpsimd", self.hmat[0:n, n:2 * n], self.hmat[0:n, 0:n], hs)
            self.dma("gpsimd", self.hmat[n:2 * n, 0:n], self.hmat[0:n, 0:n], hs)
            self.dma("gpsimd", self.hmat[n:2 * n, n:2 * n], self.hneg[0:n, 0:n], hs)
            self.wait("gpsimd", hs)
            self.dma("gpsimd", self.hneg[0:2 * n, 0:2 * n], self.hmat[0:2 * n, 0:2 * n], hs)
            self.wait("gpsimd", hs)
            self.eG.n += 1
            self.on("gpsimd", lambda g, n=n, h=self.eG.h: g.tensor_scalar_mul(
                self.hneg[0:2 * n, 0:2 * n], self.hneg[0:2 * n, 0:2 * n], -1.0).then_inc(h, 1))
            self.wait("gpsimd", self.eG)
        self.dma("gpsimd", self.upsc[:, :], self.upsc_in[:, :], self.dG)
        self.dma("gpsimd", self.gtsc[:, :], self.gtsc_in[:, :], self.dG)
        self.dma("gpsimd", self.dnsc[:, :], self.dnsc_in[:, :], self.dG)
        self.setup_dG = self.dG.n
        self.setup_eG = self.eG.n

    def bcast_row(self, row_ap, ps_ap, banks):
        self.ps_guard("tensor", banks)
        return self.step("tensor", self.eP, lambda t, r=row_ap, o=ps_ap:
                         t.matmul(o, self.ones_r[:, :], r, start=True, stop=True))

    def row_to_col(self, row_slice, ps_ap, banks):
        self.ps_guard("tensor", banks)
        return self.step("tensor", self.eP, lambda t, r=row_slice, o=ps_ap:
                         t.matmul(o, r, self.one11[:, :], start=True, stop=True))

    def phase_attention(self):
        cfg = self.cfg
        DM, TT, Q, DCH, NG, NJ, SCH, NT = (cfg.DM, cfg.TT, cfg.Q, cfg.DCH,
                                           cfg.NG, cfg.NJ, cfg.SCH, cfg.NT)
        pha = contextlib.ExitStack()
        sb = lambda n, s, d: self.sbuf(n, s, d, pha)
        with pha:
            XT = [sb("XT0", [128, DCH * 512], BF16), sb("XT1", [128, DCH * 512], BF16)]
            self.masks = sb("masks", [128, 4 * 512], BF16)
            self.kval_r = sb("kval_r", [1, TT], BF16)
            self.quiet("gpsimd", lambda g: g.memset(self.masks[:, :], 1.0))
            for r in range(4):
                self.quiet("gpsimd", lambda g, r=r: g.affine_select(
                    out=self.masks[:, r * 512:(r + 1) * 512],
                    in_=self.masks[:, r * 512:(r + 1) * 512],
                    compare_op=ALU.is_ge, fill=0.0, base=-128 * r,
                    pattern=[[1, 512]], channel_multiplier=-1))
            self.eG.n += 1
            self.on("gpsimd", lambda g, h=self.eG.h: g.engine_nop().then_inc(h, 1))
            self.setup_eG = self.eG.n
            self.dma("gpsimd", self.kval_r[:, :], self.kvalid[:, :], self.dG)
            self.setup_dG = self.dG.n
            KG = sb("KG", [128, 4 * TT], BF16)
            VG = sb("VG", [128, SCH * 512], BF16)
            QG = sb("QG", [128, 4 * Q], BF16)
            APB = sb("APB", [128, TT], BF16)
            BPB = sb("BPB", [128, TT], BF16)
            PT = [sb("PT0", [128, 1024], BF16), sb("PT1", [128, 1024], BF16),
                  sb("PT2", [128, 1024], BF16)]
            DACC = [sb("DACC0", [128, 512], F32), sb("DACC1", [128, 512], F32)]
            SMB = sb("SMB", [128, 512], BF16)
            WKB = sb("WKB", [128, DCH * 512], BF16)
            WVB = sb("WVB", [128, DCH * 512], BF16)
            WQB = sb("WQB", [128, DCH * 512], BF16)
            rowsA = sb("rowsA", [1, 1536], F32)
            colk = sb("colk", [128, 8], F32)
            colvb = sb("colvb", [128, 512], BF16)
            rdenb = [sb("rden0", [128, 512], BF16), sb("rden1", [128, 512], BF16)]
            DTS = [sb("DTS0", [128, 512], BF16), colvb]
            t1s = sb("t1s", [128, 512], F32)
            aost = [sb("aost0", [128, 512], BF16), sb("aost1", [128, 512], BF16)]
            abcols = sb("abcols", [128, 2 * SCH], F32)
            SQ = WQB

            eP, eA, eV, eG, dG, dS = self.eP, self.eA, self.eV, self.eG, self.dG, self.dS
            ln_eV = {}
            alcol_eV = {}
            fix_eV = {}
            x_dma = {}
            xsrc = self.xt.rearrange("(c p) (n t) -> p c n t", p=128, t=512)

            def load_x(tt, bi):
                x_dma[(tt, bi)] = self.dma(
                    "gpsimd", XT[bi][:, :].rearrange("p (c t) -> p c t", t=512),
                    xsrc[:, :, tt, :], dG)

            def load_w(dram, buf):
                src = dram.rearrange("(c p) o -> p c o", p=128)
                return self.dma("gpsimd", buf[:, :].rearrange("p (c o) -> p c o", o=512),
                                src[:, :, :], dG)

            def emit_stats(tt, xb, xdma):
                self.wait("gpsimd", dG, xdma)
                vg = self.eG.n + 1
                self.eG.n = vg
                self.on("gpsimd", lambda g, x=xb, h=eG.h: g.tensor_tensor(
                    SQ[:, :], x[:, :], x[:, :], ALU.mult).then_inc(h, 1))
                self.wait("tensor", dG, xdma)
                self.ps_guard("tensor", ["S4", "S5"])
                for c in range(DCH):
                    self.quiet("tensor", lambda t, c=c, x=xb: t.matmul(
                        self.PS_S[0:1, 2048:2560], self.ones_c[:, :], x[:, c * 512:(c + 1) * 512],
                        start=(c == 0), stop=(c == DCH - 1)))
                self.wait("tensor", eG, vg)
                v2 = None
                for c in range(DCH):
                    v2 = self.step("tensor", eP, lambda t, c=c: t.matmul(
                        self.PS_S[0:1, 2560:3072], self.ones_c[:, :], SQ[:, c * 512:(c + 1) * 512],
                        start=(c == 0), stop=(c == DCH - 1)))
                self.wait("vector", eP, v2)
                self.quiet("vector", lambda v: v.tensor_scalar_mul(
                    rowsA[:, 0:512], self.PS_S[0:1, 2048:2560], 1.0 / DM))
                self.quiet("vector", lambda v: v.tensor_scalar_mul(
                    rowsA[:, 512:1024], self.PS_S[0:1, 2560:3072], 1.0 / DM))
                self.quiet("vector", lambda v: v.tensor_tensor(
                    t1s[0:1, 0:512], rowsA[:, 0:512], rowsA[:, 0:512], ALU.mult))
                vv = self.step("vector", eV, lambda v: v.tensor_sub(
                    rowsA[:, 512:1024], rowsA[:, 512:1024], t1s[0:1, 0:512]))
                self.ps_release(["S4", "S5"], eV, vv)
                self.wait("scalar", eV, vv)
                self.quiet("scalar", lambda s: s.activation(
                    rowsA[:, 512:1024], rowsA[:, 512:1024], AF.Ln, bias=self.epsc[0:1, 0:1], scale=1.0))
                va = self.step("scalar", eA, lambda s: s.activation(
                    rowsA[:, 512:1024], rowsA[:, 512:1024], AF.Exp, bias=0.0, scale=-0.5))
                self.wait("vector", eA, va)
                self.wait("vector", dG, self.setup_dG)
                self.quiet("vector", lambda v: v.scalar_tensor_tensor(
                    t1s[0:1, 0:512], rowsA[:, 0:512], -1.0, rowsA[:, 512:1024],
                    op0=ALU.mult, op1=ALU.mult))
                self.quiet("vector", lambda v, tt=tt: v.tensor_tensor(
                    rowsA[:, 0:512], rowsA[:, 512:1024],
                    self.kval_r[:, tt * 512:(tt + 1) * 512], ALU.mult))
                vb = self.step("vector", eV, lambda v, tt=tt: v.tensor_tensor(
                    rowsA[:, 1024:1536], t1s[0:1, 0:512],
                    self.kval_r[:, tt * 512:(tt + 1) * 512], ALU.mult))
                self.wait("tensor", eA, va)
                self.wait("tensor", eV, vb)
                for sub in range(4):
                    scn = tt * 4 + sub
                    v1 = self.row_to_col(rowsA[:, 512 + sub * 128:512 + (sub + 1) * 128],
                                         self.PS_AV[:, 0:1], ["AV0"])
                    v1 = self.row_to_col(t1s[0:1, sub * 128:(sub + 1) * 128],
                                         self.PS_AV[:, 1:2], ["AV0"])
                    self.wait("vector", eP, v1)
                    alcol_eV[scn] = self.step("vector", eV, lambda v, scn=scn: v.tensor_copy(
                        abcols[:, 2 * scn:2 * scn + 2], self.PS_AV[:, 0:2]))
                    self.ps_release(["AV0"], eV, alcol_eV[scn])
                self.wait("tensor", eV, vb)
                p1 = self.bcast_row(rowsA[:, 0:512], self.PS_S[:, 0:512], ["S0"])
                self.wait("vector", eP, p1)
                c1 = self.step("vector", eV, lambda v, tt=tt: v.tensor_copy(
                    APB[:, tt * 512:(tt + 1) * 512], self.PS_S[:, 0:512]))
                self.ps_release(["S0"], eV, c1)
                p2 = self.bcast_row(rowsA[:, 1024:1536], self.PS_S[:, 512:1024], ["S1"])
                self.wait("vector", eP, p2)
                c2 = self.step("vector", eV, lambda v, tt=tt: v.tensor_copy(
                    BPB[:, tt * 512:(tt + 1) * 512], self.PS_S[:, 512:1024]))
                self.ps_release(["S1"], eV, c2)
                ln_eV[tt] = max(c1, c2)

            def colsum_cols(wbuf, dst_off):
                for dkc in range(4):
                    self.ps_guard("tensor", ["S0"])
                    vcs = None
                    for c in range(DCH):
                        vcs = self.step("tensor", eP, lambda t, c=c, dkc=dkc: t.matmul(
                            self.PS_S[:, 0:1],
                            wbuf[:, c * 512 + dkc * 128:c * 512 + (dkc + 1) * 128],
                            self.ones_c[:, :], start=(c == 0), stop=(c == DCH - 1)))
                    self.wait("vector", eP, vcs)
                    vc = self.step("vector", eV, lambda v, dkc=dkc, o=dst_off: v.tensor_copy(
                        colk[:, o + dkc:o + dkc + 1], self.PS_S[:, 0:1]))
                    self.ps_release(["S0"], eV, vc)

            def proj_fix_feature_major(slot_banks, psum_slice, dst, tt_or_j, colcol):
                self.wait("vector", eV, ln_eV[tt_or_j])
                self.quiet("vector", lambda v, ps=psum_slice, tt=tt_or_j: v.tensor_tensor(
                    t1s[:, :], ps, APB[:, tt * 512:(tt + 1) * 512], ALU.mult))
                vfix = self.step("vector", eV, lambda v, d=dst, tt=tt_or_j, cc=colcol:
                                 v.scalar_tensor_tensor(
                                     d, BPB[:, tt * 512:(tt + 1) * 512], cc,
                                     t1s[:, :], op0=ALU.mult, op1=ALU.add))
                self.ps_release(slot_banks, eV, vfix)
                return vfix

            wkd = wvd = wqd = 0
            kvq_done = [0] * NG
            SBANK = ["S0", "S1", "S2", "S3", "S4", "S5"]
            hj = 0
            sfree_a = {}
            sfree_v = {}
            pfree = {}
            pt_last = {}
            pt_last_v = {}
            smb_last = {}
            ao_store = {}

            def emit_av_denom(hjk, b, hl, j, par, first, exp_a, exp_v):
                nb = SCH // 2
                last = (b == nb - 1)
                slot = b % 3
                self.wait("tensor", eA, exp_a)
                if exp_v:
                    self.wait("tensor", eV, exp_v)
                if first:
                    self.ps_guard("tensor", ["AV%d" % par, "D%d" % par])
                vav = None
                for ci, c in enumerate((2 * b, 2 * b + 1)):
                    vav = self.step("tensor", eP, lambda t, c=c, ci=ci, hl=hl, slot=slot, par=par:
                                    t.matmul(
                        self.PS_AV[:, par * 512:(par + 1) * 512],
                        VG[:, c * 512 + hl * 128:c * 512 + (hl + 1) * 128],
                        PT[slot][:, ci * 512:(ci + 1) * 512],
                        start=(first and ci == 0), stop=(last and ci == 1)))
                self.wait("vector", eA, exp_a)
                if exp_v:
                    self.wait("vector", eV, exp_v)
                ts = b % 2
                vp = self.step("vector", eV, lambda v, slot=slot, ts=ts: v.tensor_add(
                    DTS[ts][:, :], PT[slot][:, 0:512], PT[slot][:, 512:1024]))
                vdn = vp
                if ts == 1:
                    self.quiet("vector", lambda v: v.tensor_add(
                        DTS[0][:, :], DTS[0][:, :], DTS[1][:, :]))
                    if b == 1:
                        self.wait("vector", eP, self.dacc_guard.get(par, 0))
                        vdn = self.step("vector", eV, lambda v, par=par: v.tensor_copy(
                            DACC[par][:, :], DTS[0][:, :]))
                    else:
                        vdn = self.step("vector", eV, lambda v, par=par: v.tensor_add(
                            DACC[par][:, :], DACC[par][:, :], DTS[0][:, :]))
                pfree[(hjk, b)] = max(vav, vdn)
                pt_last[slot] = vav
                pt_last_v[slot] = vp
                return vav, vdn

            self.pending_finish = None
            self.dacc_guard = {}

            def finish_hj(par, g, hl, j, vav, vdn):
                self.ps_guard("tensor", ["S5"])
                self.wait("tensor", eV, vdn)
                vsum = self.step("tensor", eP, lambda t, par=par: t.matmul(
                    self.PS_S[0:1, 2560:3072],
                    self.ones_cf[:, :], DACC[par][:, :], start=True, stop=True))
                self.wait("scalar", eP, vsum)
                self.quiet("scalar", lambda s: s.activation(
                    rowsA[0:1, 0:512], self.PS_S[0:1, 2560:3072],
                    AF.Ln, bias=0.0, scale=1.0))
                rrec = self.step("scalar", eA, lambda s: s.activation(
                    rowsA[0:1, 0:512], rowsA[0:1, 0:512],
                    AF.Exp, bias=0.0, scale=-1.0))
                self.wait("tensor", eA, rrec)
                pbc = self.bcast_row(rowsA[0:1, 0:512],
                                     self.PS_S[:, 2560:3072], [])
                self.wait("vector", eP, pbc)
                vrd = self.step("vector", eV, lambda v, par=par: v.tensor_copy(
                    rdenb[par][:, :], self.PS_S[:, 2560:3072]))
                self.ps_release(["S5"], eV, vrd)
                self.wait("vector", eP, vav)
                if par in ao_store:
                    self.wait("vector", dS, ao_store[par])
                vev = self.step("vector", eV, lambda v, par=par: v.tensor_mul(
                    aost[par][:, :], self.PS_AV[:, par * 512:(par + 1) * 512],
                    rdenb[par][:, :]))
                self.ps_release(["AV%d" % par], eV, vev)
                self.wait("sync", eV, vev)
                ch = g * 4 + hl
                ao_store[par] = self.dma(
                    "sync", self.ao_dram[ch, :, j * 512:(j + 1) * 512],
                    aost[par][:, :], dS)
                self.dacc_guard[par] = vsum

            for g in range(NG):
                if g > 0:
                    self.wait("gpsimd", eP, kvq_done[g - 1])
                wkd = load_w(self.wk[:, g * 512:(g + 1) * 512], WKB)
                wvd = load_w(self.wv[:, g * 512:(g + 1) * 512], WVB)
                self.wait("tensor", dG, wkd)
                colsum_cols(WKB, 0)
                self.wait("tensor", dG, wvd)
                self.ps_guard("tensor", ["S1"])
                vcs = None
                for c in range(DCH):
                    vcs = self.step("tensor", eP, lambda t, c=c: t.matmul(
                        self.PS_S[0:1, 512:1024], self.ones_c[:, :], WVB[:, c * 512:(c + 1) * 512],
                        start=(c == 0), stop=(c == DCH - 1)))
                self.wait("vector", eP, vcs)
                vr = self.step("vector", eV, lambda v: v.tensor_copy(
                    t1s[0:1, 0:512], self.PS_S[0:1, 512:1024]))
                self.ps_release(["S1"], eV, vr)
                self.wait("tensor", eV, vr)
                pb = self.bcast_row(t1s[0:1, 0:512], self.PS_S[:, 0:512], ["S0"])
                self.wait("vector", eP, pb)
                cvb_eV = self.step("vector", eV, lambda v: v.tensor_copy(
                    colvb[:, :], self.PS_S[:, 0:512]))
                self.ps_release(["S0"], eV, cvb_eV)

                x_used = {}
                for pos in range(NT):
                    tt = pos
                    bi = pos % 2
                    if pos >= 2:
                        self.wait("gpsimd", eP, x_used[pos - 2])
                    load_x(tt, bi)
                    xb = XT[bi]
                    if g == 0:
                        emit_stats(tt, xb, x_dma[(tt, bi)])
                    self.wait("tensor", dG, max(x_dma[(tt, bi)], wkd))
                    for dkc in range(4):
                        sl = dkc % 2
                        self.ps_guard("tensor", [SBANK[sl]])
                        vl = None
                        for c in range(DCH):
                            vl = self.step("tensor", eP, lambda t, c=c, dkc=dkc, sl=sl, xb=xb:
                                           t.matmul(
                                self.PS_S[:, sl * 512:(sl + 1) * 512],
                                WKB[:, c * 512 + dkc * 128:c * 512 + (dkc + 1) * 128],
                                xb[:, c * 512:(c + 1) * 512],
                                start=(c == 0), stop=(c == DCH - 1)))
                        self.wait("vector", eP, vl)
                        proj_fix_feature_major(
                            [SBANK[sl]], self.PS_S[:, sl * 512:(sl + 1) * 512],
                            KG[:, dkc * TT + tt * 512:dkc * TT + (tt + 1) * 512],
                            tt, colk[:, dkc:dkc + 1])
                    self.wait("tensor", dG, wvd)
                    self.wait("tensor", eV, cvb_eV)
                    for sub in range(4):
                        scn = tt * 4 + sub
                        sl = 2 + sub % 2
                        self.ps_guard("tensor", [SBANK[sl]])
                        vl = None
                        for c in range(DCH):
                            vl = self.step("tensor", eP, lambda t, c=c, sub=sub, sl=sl, xb=xb:
                                           t.matmul(
                                self.PS_S[:, sl * 512:(sl + 1) * 512],
                                xb[:, c * 512 + sub * 128:c * 512 + (sub + 1) * 128],
                                WVB[:, c * 512:(c + 1) * 512],
                                start=(c == 0), stop=(c == DCH - 1)))
                        self.wait("vector", eP, vl)
                        self.wait("vector", eV, alcol_eV[scn])
                        self.quiet("vector", lambda v, scn=scn, sl=sl: v.tensor_scalar(
                            t1s[:, :], self.PS_S[:, sl * 512:(sl + 1) * 512],
                            abcols[:, 2 * scn:2 * scn + 1], None, op0=ALU.mult))
                        vfx = self.step("vector", eV, lambda v, scn=scn: v.scalar_tensor_tensor(
                            VG[:, scn * 512:(scn + 1) * 512],
                            colvb[:, :], abcols[:, 2 * scn + 1:2 * scn + 2],
                            t1s[:, :], op0=ALU.mult, op1=ALU.add))
                        self.ps_release([SBANK[sl]], eV, vfx)
                    x_used[pos] = eP.n

                self.wait("gpsimd", eP, eP.n)
                wqd = load_w(self.wq[:, g * 512:(g + 1) * 512], WQB)
                for j in range(NJ):
                    load_x(j, j % 2)
                self.wait("tensor", dG, wqd)
                colsum_cols(WQB, 4)
                for j in range(NJ):
                    xb = XT[j % 2]
                    self.wait("tensor", dG, x_dma[(j, j % 2)])
                    for dkc in range(4):
                        sl = dkc % 2
                        self.ps_guard("tensor", [SBANK[sl]])
                        vl = None
                        for c in range(DCH):
                            vl = self.step("tensor", eP, lambda t, c=c, dkc=dkc, sl=sl, xb=xb:
                                           t.matmul(
                                self.PS_S[:, sl * 512:(sl + 1) * 512],
                                WQB[:, c * 512 + dkc * 128:c * 512 + (dkc + 1) * 128],
                                xb[:, c * 512:(c + 1) * 512],
                                start=(c == 0), stop=(c == DCH - 1)))
                        self.wait("vector", eP, vl)
                        proj_fix_feature_major(
                            [SBANK[sl]], self.PS_S[:, sl * 512:(sl + 1) * 512],
                            QG[:, dkc * Q + j * 512:dkc * Q + (j + 1) * 512],
                            j, colk[:, 4 + dkc:5 + dkc])
                kvq_done[g] = eP.n
                qfix_eV = eV.n

                for hl in range(4):
                    for j in range(NJ):
                        par = hj % 2
                        nb = SCH // 2
                        self.wait("tensor", eV, qfix_eV)
                        prev = None
                        first = True
                        for b in range(nb):
                            if b == 3 and self.pending_finish is not None:
                                self.pending_finish()
                                self.pending_finish = None
                            c0, c1 = 2 * b, 2 * b + 1
                            kinds = (mask_kind(cfg, j, c0), mask_kind(cfg, j, c1))
                            slot = b % 3
                            banks = ["S%d" % (2 * slot), "S%d" % (2 * slot + 1)]
                            self.ps_guard("tensor", banks)
                            if slot in pt_last:
                                self.wait("scalar", eP, pt_last[slot])
                                self.wait("vector", eP, pt_last[slot])
                                self.wait("scalar", eV, pt_last_v[slot])
                                self.wait("vector", eV, pt_last_v[slot])
                            smm = None
                            for ci, c in enumerate((c0, c1)):
                                if kinds[ci] == "zero":
                                    continue
                                smm = self.step("tensor", eP, lambda t, hl=hl, j=j, c=c,
                                                slot=slot, ci=ci: t.matmul(
                                    self.PS_S[:, slot * 1024 + ci * 512:slot * 1024 + (ci + 1) * 512],
                                    KG[:, hl * TT + c * 128:hl * TT + (c + 1) * 128],
                                    QG[:, hl * Q + j * 512:hl * Q + (j + 1) * 512],
                                    start=True, stop=True))
                            ea = ev = 0
                            if kinds == ("ones", "ones"):
                                self.wait("scalar", eP, smm)
                                ea = self.step("scalar", eA, lambda s, slot=slot: s.activation(
                                    PT[slot][:, :], self.PS_S[:, slot * 1024:(slot + 1) * 1024],
                                    AF.Exp, bias=self.negC[:, 0:1], scale=cfg.att_scale))
                                self.ps_release(banks, eA, ea)
                            else:
                                for ci, c in enumerate((c0, c1)):
                                    kk = kinds[ci]
                                    if kk == "zero":
                                        ev = self.step("vector", eV, lambda v, slot=slot, ci=ci:
                                                       v.memset(PT[slot][:, ci * 512:(ci + 1) * 512],
                                                                cfg.expC))
                                    elif kk == "ones":
                                        self.wait("scalar", eP, smm)
                                        ea = self.step("scalar", eA, lambda s, slot=slot, ci=ci:
                                                       s.activation(
                                            PT[slot][:, ci * 512:(ci + 1) * 512],
                                            self.PS_S[:, slot * 1024 + ci * 512:slot * 1024 + (ci + 1) * 512],
                                            AF.Exp, bias=self.negC[:, 0:1], scale=cfg.att_scale))
                                        self.ps_release(banks[ci:ci + 1], eA, ea)
                                    else:
                                        r = int(kk)
                                        self.wait("vector", eP, smm)
                                        self.wait("vector", eG, self.setup_eG)
                                        if 0 in smb_last:
                                            self.wait("vector", eA, smb_last[0])
                                        vm = self.step("vector", eV, lambda v, slot=slot, ci=ci, r=r:
                                                       v.tensor_tensor(
                                            SMB[:, 0:512],
                                            self.PS_S[:, slot * 1024 + ci * 512:slot * 1024 + (ci + 1) * 512],
                                            self.masks[:, r * 512:(r + 1) * 512], ALU.mult))
                                        self.ps_release(banks[ci:ci + 1], eV, vm)
                                        self.wait("scalar", eV, vm)
                                        ea = self.step("scalar", eA, lambda s, slot=slot, ci=ci:
                                                       s.activation(
                                            PT[slot][:, ci * 512:(ci + 1) * 512],
                                            SMB[:, 0:512],
                                            AF.Exp, bias=self.negC[:, 0:1], scale=cfg.att_scale))
                                        smb_last[0] = ea
                            sfree_a[(hj, b)] = ea
                            sfree_v[(hj, b)] = ev
                            if prev is not None:
                                emit_av_denom(hj, prev, hl, j, par, first,
                                              sfree_a[(hj, prev)], sfree_v[(hj, prev)])
                                first = False
                            prev = b
                        vav, vdn = emit_av_denom(hj, prev, hl, j, par, first,
                                                 sfree_a[(hj, prev)], sfree_v[(hj, prev)])
                        self.pending_finish = (lambda par=par, g=g, hl=hl, j=j,
                                               vav=vav, vdn=vdn:
                                               finish_hj(par, g, hl, j, vav, vdn))
                        hj += 1

                if self.pending_finish is not None:
                    self.pending_finish()
                    self.pending_finish = None
            self.wait("sync", dS)

    def emit_butterflies(self, hv, lo_chunks, hi_chunks, BFA, BFB, tq, ttot,
                         ts_list=None):
        eV = self.eV
        n = hi_chunks - lo_chunks
        nst = int(math.log2(n))
        if nst == 1:
            self.emit_cross_butterfly(hv[:, lo_chunks:hi_chunks, :], 1, 0, BFA, tq, ttot,
                                      ts_list=ts_list)
            return
        for ts in (range(ttot // tq) if ts_list is None else ts_list):
            cur = hv[:, lo_chunks:hi_chunks, ts * tq:(ts + 1) * tq]
            prevb = None
            for st in range(nst):
                src_ = cur if st == 0 else prevb
                dstbuf = (BFA if st % 2 == 0 else BFB)
                dst = dstbuf[:, 0:n * tq].rearrange("p (c q) -> p c q", q=tq)
                if st == nst - 1:
                    dst = cur
                h = 1 << st
                sv = src_.rearrange("p (bk two h) q -> p bk two h q", two=2, h=h)
                dv = dst.rearrange("p (bk two h) q -> p bk two h q", two=2, h=h)
                self.quiet("vector", lambda v, sv=sv, dv=dv: v.tensor_add(
                    dv[:, :, 0, :, :], sv[:, :, 0, :, :], sv[:, :, 1, :, :]))
                self.step("vector", eV, lambda v, sv=sv, dv=dv: v.tensor_sub(
                    dv[:, :, 1, :, :], sv[:, :, 0, :, :], sv[:, :, 1, :, :]))
                prevb = dst

    def emit_cross_butterfly(self, hv, dist, block_lo, BFS, tq, ttot, BFS2=None,
                             ts_list=None):
        eV = self.eV
        a = hv[:, block_lo:block_lo + dist, :]
        b = hv[:, block_lo + dist:block_lo + 2 * dist, :]
        sc = BFS[:, 0:dist * tq].rearrange("p (c q) -> p c q", q=tq)
        if BFS2 is None:
            BFS2 = BFS
            half = dist * tq
        else:
            half = 0
        sc2 = BFS2[:, half:half + dist * tq].rearrange("p (c q) -> p c q", q=tq) \
            if half else BFS2[:, 0:dist * tq].rearrange("p (c q) -> p c q", q=tq)
        for ts in (range(ttot // tq) if ts_list is None else ts_list):
            asl = a[:, :, ts * tq:(ts + 1) * tq]
            bsl = b[:, :, ts * tq:(ts + 1) * tq]
            self.quiet("vector", lambda v, asl=asl, bsl=bsl, sc=sc: v.tensor_sub(
                sc, asl, bsl))
            self.quiet("vector", lambda v, asl=asl, bsl=bsl, sc2=sc2: v.tensor_add(
                sc2, asl, bsl))
            self.quiet("vector", lambda v, asl=asl, sc2=sc2: v.tensor_copy(asl, sc2))
            self.step("vector", eV, lambda v, bsl=bsl, sc=sc: v.tensor_copy(bsl, sc))

    def butterfly_triggers(self, hv, done_chunks, total, QC, BFA, BFB, tq, ttot):
        if done_chunks % QC == 0:
            self.emit_butterflies(hv, done_chunks - QC, done_chunks, BFA, BFB, tq, ttot)
        d = 2 * QC
        while d <= total and done_chunks % d == 0:
            self.emit_cross_butterfly(hv, d // 2, done_chunks - d, BFA, tq, ttot)
            d *= 2


    def phase_wo_ln2(self):
        cfg = self.cfg
        DM, Q, DCH, NJ = cfg.DM, cfg.Q, cfg.DCH, cfg.NJ
        eP, eA, eV, eG, dG, dS = self.eP, self.eA, self.eV, self.eG, self.dG, self.dS
        X1 = self.X1
        QCB = max(DCH // 4, 2)
        phb = contextlib.ExitStack()
        sb = lambda n, s, d: self.sbuf(n, s, d, phb)
        with phb:
            AOR = [sb("AOR0", [128, DCH * 512], BF16), sb("AOR1", [128, DCH * 512], BF16)]
            WOB = [sb("WOB0", [128, DCH * 128], BF16), sb("WOB1", [128, DCH * 128], BF16)]
            XOW = [sb("XOW0", [128, 512], F32), sb("XOW1", [128, 512], F32)]
            XMF = [sb("XMF0", [128, 512], F32), sb("XMF1", [128, 512], F32)]
            XB = [sb("XB0", [128, 512], BF16), sb("XB1", [128, 512], BF16)]
            SQT = [sb("SQT0", [128, 512], BF16), sb("SQT1", [128, 512], BF16)]
            XN = [sb("XN0", [128, 512], BF16), sb("XN1", [128, 512], BF16)]
            t2s = sb("t2s", [1, 512], F32)
            t2f = sb("t2f", [128, 512], F32)
            self.rows2 = self.sbuf("rows2", [1, 2 * cfg.Q], F32, phb)
            sucol = sb("sucol", [128, DM // 128], BF16)
            sucolf = sb("sucolf", [128, DM // 128], F32)
            HSU = sb("HSU", [128, DM // 128], F32)
            A2B = sb("A2B", [128, 512], F32)
            B2B = sb("B2B", [128, 512], F32)
            BFA = sb("BFAw", [128, DCH * 128], BF16)
            BFB = sb("BFBw", [128, max(QCB, 2) * 128], BF16)

            sud = self.dma("gpsimd", sucol[:, :], self.suug[:, :], dG)
            self.dma("gpsimd", sucolf[:, :], self.suug[:, :], dG)
            sud = dG.n
            self.wait("tensor", dG, sud)
            self.wait("tensor", eG, self.setup_eG)
            self.ps_guard("tensor", ["S2"])
            vhs = None
            for cp in range(DCH):
                for c in range(DCH):
                    neg = bin(c & cp).count("1") % 2 == 1
                    mat = self.hneg if neg else self.hmat
                    vhs = self.step("tensor", eP, lambda t, cp=cp, c=c, mat=mat: t.matmul(
                        self.PS_S[:, 1024 + cp:1025 + cp], mat[:, :],
                        sucol[:, c:c + 1],
                        start=(c == 0), stop=(c == DCH - 1)))
            self.wait("vector", eP, vhs)
            vhs2 = self.step("vector", eV, lambda v: v.tensor_copy(
                HSU[:, :], self.PS_S[:, 1024:1024 + DCH]))
            self.ps_release(["S2"], eV, vhs2)

            ao_ch = [0] * NJ
            for j2 in range(NJ):
                ao_ch[j2] = self.dma(
                    "gpsimd", AOR[j2][:, :].rearrange("p (c t) -> p c t", t=512),
                    self.ao_dram[:, :, j2 * 512:(j2 + 1) * 512].rearrange("c p t -> p c t"),
                    dG)
            xow_used = {}
            xmf_used = {}
            wo_used = {}
            xn_used = {}
            xb_used = {}
            sqt_used = {}
            self.x1_half = [0] * NJ
            x1v = X1[:, :].rearrange("p (c q) -> p c q", q=Q)
            for j2 in range(NJ):
                self.wait("tensor", dG, ao_ch[j2])
                x1j = x1v[:, :, j2 * 512:(j2 + 1) * 512]
                for oc in range(DCH):
                    bi = oc % 2
                    key = j2 * DCH + oc
                    if key >= 2:
                        self.wait("gpsimd", eP, wo_used[key - 2])
                    wod = self.dma("gpsimd",
                                   WOB[bi][:, :].rearrange("p (c o) -> p c o", o=128),
                                   self.wo[oc, :, :, :].rearrange("c p o -> p c o"), dG)
                    if key - 2 in xow_used:
                        self.wait("sync", eV, xow_used[key - 2])
                    xd = self.dma("sync", XOW[bi][:, :],
                                  self.xt[oc * 128:(oc + 1) * 128, j2 * 512:(j2 + 1) * 512],
                                  dS)
                    self.wait("tensor", dG, wod)
                    sl = oc % 2
                    self.ps_guard("tensor", ["S%d" % sl])
                    vl = None
                    for dc in range(DCH):
                        vl = self.step("tensor", eP, lambda t, dc=dc, bi=bi, sl=sl, j2=j2:
                                       t.matmul(
                            self.PS_S[:, sl * 512:(sl + 1) * 512],
                            WOB[bi][:, dc * 128:(dc + 1) * 128],
                            AOR[j2][:, dc * 512:(dc + 1) * 512],
                            start=(dc == 0), stop=(dc == DCH - 1)))
                    wo_used[key] = eP.n
                    self.wait("vector", eP, vl)
                    self.wait("vector", dS, xd)
                    if key - 2 in xmf_used:
                        self.wait("vector", dS, xmf_used[key - 2][0])
                        self.wait("vector", eA, xmf_used[key - 2][1])
                    vmf = self.step("vector", eV, lambda v, sl=sl, bi=bi: v.tensor_add(
                        XMF[bi][:, :], self.PS_S[:, sl * 512:(sl + 1) * 512],
                        XOW[bi][:, :]))
                    self.ps_release(["S%d" % sl], eV, vmf)
                    xow_used[key] = vmf
                    self.wait("sync", eV, vmf)
                    std = self.dma("sync",
                                   self.xmid_dram[oc, :, j2 * 512:(j2 + 1) * 512],
                                   XMF[bi][:, :], dS)
                    self.wait("scalar", eV, vmf)
                    if key - 2 in xb_used:
                        self.wait("scalar", eP, xb_used[key - 2][0])
                        self.wait("scalar", eV, xb_used[key - 2][1])
                    vxb = self.step("scalar", eA, lambda s, bi=bi: s.activation(
                        XB[bi][:, :], XMF[bi][:, :], AF.Copy, bias=0.0, scale=1.0))
                    xmf_used[key] = (std, vxb)
                    if key - 2 in sqt_used:
                        self.wait("vector", eP, sqt_used[key - 2])
                    vsq = self.step("vector", eV, lambda v, bi=bi: v.tensor_mul(
                        SQT[bi][:, :], XMF[bi][:, :], XMF[bi][:, :]))
                    self.wait("tensor", eA, vxb)
                    if oc == 0:
                        self.ps_guard("tensor", ["S4", "S5"])
                    vsx = self.step("tensor", eP, lambda t, oc=oc, bi=bi: t.matmul(
                        self.PS_S[0:1, 2048:2560], self.ones_c[:, :], XB[bi][:, :],
                        start=(oc == 0), stop=(oc == DCH - 1)))
                    self.wait("tensor", eV, vsq)
                    v2 = self.step("tensor", eP, lambda t, oc=oc, bi=bi: t.matmul(
                        self.PS_S[0:1, 2560:3072], self.ones_c[:, :], SQT[bi][:, :],
                        start=(oc == 0), stop=(oc == DCH - 1)))
                    sqt_used[key] = v2
                    if bi in xn_used:
                        self.wait("vector", eP, xn_used[bi])
                    self.wait("vector", eA, vxb)
                    vxn = self.step("vector", eV, lambda v, bi=bi, oc=oc: v.tensor_scalar(
                        XN[bi][:, :], XB[bi][:, :], sucolf[:, oc:oc + 1], None,
                        op0=ALU.mult))
                    xb_used[key] = (vsx, vxn)
                    sl2 = 2 + oc % 2
                    self.ps_guard("tensor", ["S%d" % sl2])
                    self.wait("tensor", eV, vxn)
                    vlo = self.step("tensor", eP, lambda t, sl2=sl2, bi=bi: t.matmul(
                        self.PS_S[:, sl2 * 512:(sl2 + 1) * 512], self.hmat[:, :],
                        XN[bi][:, :], start=True, stop=True))
                    xn_used[bi] = vlo
                    self.wait("vector", eP, vlo)
                    vx1 = self.step("vector", eV, lambda v, sl2=sl2, oc=oc, j2=j2:
                                    v.tensor_copy(
                        X1[:, oc * Q + j2 * 512:oc * Q + (j2 + 1) * 512],
                        self.PS_S[:, sl2 * 512:(sl2 + 1) * 512]))
                    self.ps_release(["S%d" % sl2], eV, vx1)
                    self.butterfly_triggers(x1j, oc + 1, DCH, QCB, BFA, BFB, 128, 512)
                self.wait("vector", eP, v2)
                a2 = self.rows2[:, j2 * 512:(j2 + 1) * 512]
                b2 = self.rows2[:, Q + j2 * 512:Q + (j2 + 1) * 512]
                self.quiet("vector", lambda v, a2=a2: v.tensor_scalar_mul(
                    a2, self.PS_S[0:1, 2048:2560], 1.0 / DM))
                self.quiet("vector", lambda v, b2=b2: v.tensor_scalar_mul(
                    b2, self.PS_S[0:1, 2560:3072], 1.0 / DM))
                self.quiet("vector", lambda v, a2=a2: v.tensor_tensor(
                    t2s[:, :], a2, a2, ALU.mult))
                vv = self.step("vector", eV, lambda v, b2=b2: v.tensor_sub(
                    b2, b2, t2s[:, :]))
                self.ps_release(["S4", "S5"], eV, vv)
                self.wait("scalar", eV, vv)
                self.quiet("scalar", lambda s, b2=b2: s.activation(
                    b2, b2, AF.Ln, bias=self.epsc[0:1, 0:1], scale=1.0))
                va2 = self.step("scalar", eA, lambda s, b2=b2: s.activation(
                    b2, b2, AF.Exp, bias=0.0, scale=-0.5))
                self.wait("vector", eA, va2)
                vb2 = self.step("vector", eV, lambda v, a2=a2, b2=b2: v.scalar_tensor_tensor(
                    t2s[:, :], a2, -1.0, b2, op0=ALU.mult, op1=ALU.mult))
                self.wait("tensor", eV, vb2)
                pa = self.bcast_row(b2, self.PS_S[:, 2048:2560], ["S4"])
                pb = self.bcast_row(t2s[0:1, :], self.PS_S[:, 2560:3072], ["S5"])
                self.wait("vector", eP, pb)
                va2b = self.step("vector", eV, lambda v: v.tensor_copy(
                    A2B[:, :], self.PS_S[:, 2048:2560]))
                vb2b = self.step("vector", eV, lambda v: v.tensor_copy(
                    B2B[:, :], self.PS_S[:, 2560:3072]))
                self.ps_release(["S4", "S5"], eV, vb2b)
                for oc in range(DCH):
                    self.quiet("vector", lambda v, oc=oc, j2=j2: v.tensor_tensor(
                        t2f[:, :], X1[:, oc * Q + j2 * 512:oc * Q + (j2 + 1) * 512],
                        A2B[:, :], ALU.mult))
                    self.step("vector", eV, lambda v, oc=oc, j2=j2: v.scalar_tensor_tensor(
                        X1[:, oc * Q + j2 * 512:oc * Q + (j2 + 1) * 512],
                        B2B[:, :], HSU[:, oc:oc + 1], t2f[:, :],
                        op0=ALU.mult, op1=ALU.add))
                self.x1_half[j2] = self.eV.n
            self.x1_ready = self.eV.n
            self.wo_tail = {"eP": eP.n, "eA": self.eA.n, "dS": dS.n, "dG": dG.n,
                            "eV": self.eV.n}
            if cfg.debug2:
                self.wait("sync", eV)
                self.dma("sync", self.hsu_out, HSU[:, :], dS)

    def phase_ffn(self):
        cfg = self.cfg
        DM, IN, Q, DCH, ICH, NJ = cfg.DM, cfg.IN, cfg.Q, cfg.DCH, cfg.ICH, cfg.NJ
        eP, eA, eV, eG, dG, dS = self.eP, self.eA, self.eV, self.eG, self.dG, self.dS
        X1 = self.X1

        HB = self.HB
        X1 = self.X1
        HB2 = self.sbuf("HB2", [128, IN // 128 * 512], BF16)

        def emit_butterflies(hbuf, lo_chunks, hi_chunks, BFA, BFB, tq, ts_list=None):
            self.emit_butterflies(
                hbuf[:, :].rearrange("p (c q) -> p c q", q=512),
                lo_chunks, hi_chunks, BFA, BFB, tq, 512, ts_list=ts_list)

        def emit_cross_butterfly(hbuf, dist, block_lo, BFS, tq, ts_list=None):
            self.emit_cross_butterfly(
                hbuf[:, :].rearrange("p (c q) -> p c q", q=512),
                dist, block_lo, BFS, tq, 512, BFS2=BFB, ts_list=ts_list)

        c2a = contextlib.ExitStack()
        with c2a:
            sb = lambda n_, s_, d_: self.sbuf(n_, s_, d_, c2a)
            UPS = [[sb("UPS%d%d" % (h_, p_), [128, 512], BF16) for p_ in range(3)]
                   for h_ in range(NJ)]
            GTS = [[sb("GTS%d%d" % (h_, p_), [128, 512], BF16) for p_ in range(3)]
                   for h_ in range(NJ)]
            QC = max(ICH // 4, 2)
            hb_ready = [0, 0]
            BFA = sb("BFA2", [128, max(ICH // 2, QC) * 128], BF16)
            BFB = sb("BFB2", [128, max(ICH // 2, QC) * 128], BF16)
            WUGB = [sb("WUGB0", [128, 2 * DCH * 128], BF16),
                    sb("WUGB1", [128, 2 * DCH * 128], BF16)]
            self.wait("scalar", eV, self.wo_tail["eV"])
            self.wait("vector", eP, self.wo_tail["eP"])
            self.wait("vector", eA, self.wo_tail["eA"])
            self.wait("vector", dS, self.wo_tail["dS"])
            self.wait("vector", dG, self.wo_tail["dG"])
            if cfg.debug:
                self.step("vector", self.eV, lambda v: v.memset(HB2[:, :], 0.0))
                wclaim = 0
                for tns in (WUGB[0], WUGB[1]):
                    wclaim = self.step("vector", self.eV,
                                       lambda v, tns=tns: v.memset(tns[:, :], 0.0))
                self.wait("gpsimd", eV, wclaim)
            upg_pe = {}
            ups_used = {}
            bfq = [[], []]
            for ocg in range(ICH):
                bi = ocg % 2
                if ocg >= 2:
                    self.wait("gpsimd", eP, upg_pe[ocg - 2])
                wud = self.dma("gpsimd",
                               WUGB[bi][:, :].rearrange("p (w c o) -> p w c o", w=2, o=128),
                               self.wug[ocg, :, :, :, :].rearrange("w c p o -> p w c o"),
                               dG)
                self.wait("tensor", dG, wud)
                for hf in range(NJ):
                    hbuf = HB if hf == 0 else HB2
                    st = 2 * ocg + hf
                    pu = ocg % 3
                    slu = st % 2
                    slg = 2 + st % 2
                    self.wait("tensor", eV, self.x1_half[hf])
                    self.ps_guard("tensor", ["S%d" % slu])
                    vu = None
                    for dc in range(DCH):
                        vu = self.step("tensor", eP, lambda t, dc=dc, slu=slu, bi=bi, hf=hf:
                                       t.matmul(
                            self.PS_S[:, slu * 512:(slu + 1) * 512],
                            WUGB[bi][:, dc * 128:(dc + 1) * 128],
                            X1[:, dc * Q + hf * 512:dc * Q + (hf + 1) * 512],
                            start=(dc == 0), stop=(dc == DCH - 1)))
                    self.ps_guard("tensor", ["S%d" % slg])
                    vg = None
                    for dc in range(DCH):
                        vg = self.step("tensor", eP, lambda t, dc=dc, slg=slg, bi=bi, hf=hf:
                                       t.matmul(
                            self.PS_S[:, slg * 512:(slg + 1) * 512],
                            WUGB[bi][:, DCH * 128 + dc * 128:DCH * 128 + (dc + 1) * 128],
                            X1[:, dc * Q + hf * 512:dc * Q + (hf + 1) * 512],
                            start=(dc == 0), stop=(dc == DCH - 1)))
                    self.wait("scalar", eP, vu)
                    if (hf, pu) in ups_used:
                        self.wait("scalar", eV, ups_used[(hf, pu)])
                    vue = self.step("scalar", eA, lambda s, slu=slu, ocg=ocg, hf=hf, pu=pu:
                                    s.activation(
                        UPS[hf][pu][:, :], self.PS_S[:, slu * 512:(slu + 1) * 512],
                        AF.Copy, bias=0.0, scale=self.upsc[:, ocg:ocg + 1]))
                    self.ps_release(["S%d" % slu], eA, vue)
                    self.wait("scalar", eP, vg)
                    vge = self.step("scalar", eA, lambda s, slg=slg, ocg=ocg, hf=hf, pu=pu:
                                    s.activation(
                        GTS[hf][pu][:, :], self.PS_S[:, slg * 512:(slg + 1) * 512],
                        AF.Silu, bias=0.0, scale=self.gtsc[:, ocg:ocg + 1]))
                    self.ps_release(["S%d" % slg], eA, vge)
                    self.wait("vector", eA, vge)
                    vh = self.step("vector", eV, lambda v, ocg=ocg, hbuf=hbuf, hf=hf, pu=pu:
                                   v.tensor_mul(
                        hbuf[:, ocg * 512:(ocg + 1) * 512], GTS[hf][pu][:, :],
                        UPS[hf][pu][:, :]))
                    ups_used[(hf, pu)] = vh
                upg_pe[ocg] = eP.n
                NTS = 4
                for hb_i, hbuf in enumerate((HB, HB2)):
                    if (ocg + 1) % QC == 0:
                        qlo = ocg + 1 - QC
                        for ts in range(NTS):
                            bfq[hb_i].append(
                                lambda ts=ts, qlo=qlo, hbuf=hbuf: emit_butterflies(
                                    hbuf, qlo, qlo + QC, BFA, BFB, 128, ts_list=[ts]))
                    d = 2 * QC
                    while d < ICH and (ocg + 1) % d == 0:
                        for ts in range(NTS):
                            bfq[hb_i].append(
                                lambda ts=ts, d=d, blo=ocg + 1 - d, hbuf=hbuf:
                                emit_cross_butterfly(hbuf, d // 2, blo, BFA, 128,
                                                     ts_list=[ts]))
                        d *= 2
                for hb_i in range(2):
                    nemit = len(bfq[hb_i]) if ocg == ICH - 1 else min(2, len(bfq[hb_i]))
                    for _ in range(nemit):
                        bfq[hb_i].pop(0)()
                    if ocg == ICH - 1:
                        hb_ready[hb_i] = self.eV.n
            c2a_eA = self.eA.n
            c2a_dG = dG.n
            c2a_eP = eP.n

        c2b = contextlib.ExitStack()
        with c2b:
            sb = lambda n_, s_, d_: self.sbuf(n_, s_, d_, c2b)
            WDB = [sb("WDB0", [128, ICH * 128], BF16), sb("WDB1", [128, ICH * 128], BF16)]
            t3p = [sb("t3c2a", [128, 512], BF16), sb("t3c2b", [128, 512], BF16)]
            XMR = [sb("XMR0", [128, 512], F32), sb("XMR1", [128, 512], F32)]
            OUTS = [sb("OUTS0", [128, 512], F32), sb("OUTS1", [128, 512], F32)]

            if cfg.debug2:
                self.wait("sync", eV, hb_ready[0])
                self.dma("sync", self.hb_out, HB[:, :], dS)
                self.dma("sync", self.x1_out, X1[:, :], dS)
                self.wait("tensor", dS)
            claim = hb_ready[NJ - 1]
            if cfg.debug:
                for tns in (WDB[0], WDB[1], t3p[0], t3p[1], XMR[0], XMR[1],
                            OUTS[0], OUTS[1]):
                    claim = self.step("vector", eV,
                                      lambda v, tns=tns: v.memset(tns[:, :], 0.0))
            self.wait("gpsimd", eA, c2a_eA)
            self.wait("gpsimd", eV, claim)
            self.wait("gpsimd", eP, c2a_eP)
            self.wait("sync", eV, claim)
            self.wait("sync", eP, c2a_eP)
            self.wait("sync", dG, c2a_dG)
            lo_eA = {}
            H2 = ICH // 2
            for hf in range(NJ):
                hbuf = HB if hf == 0 else HB2
                self.wait("tensor", eV, hb_ready[hf])
                for c in range(H2):
                    sl = (c % 2) * 2
                    self.ps_guard("tensor", ["S%d" % sl, "S%d" % (sl + 1)])
                    for mat, off in ((self.hmat, 0), (self.hneg, 512)):
                        self.quiet("tensor", lambda t, c=c, sl=sl, hbuf=hbuf, off=off:
                                   t.matmul(
                            self.PS_S[:, sl * 512 + off:sl * 512 + off + 512],
                            self.hmat[:, :], hbuf[:, c * 512:(c + 1) * 512],
                            start=True, stop=False))
                    vmm = None
                    for mat, off in ((self.hmat, 0), (self.hneg, 512)):
                        vmm = self.step("tensor", eP, lambda t, c=c, sl=sl, hbuf=hbuf,
                                        mat=mat, off=off: t.matmul(
                            self.PS_S[:, sl * 512 + off:sl * 512 + off + 512],
                            mat[:, :], hbuf[:, (c + H2) * 512:(c + H2 + 1) * 512],
                            start=False, stop=True))
                    self.wait("scalar", eP, vmm)
                    self.quiet("scalar", lambda s, c=c, sl=sl, hbuf=hbuf: s.activation(
                        hbuf[:, c * 512:(c + 1) * 512],
                        self.PS_S[:, sl * 512:sl * 512 + 512],
                        AF.Copy, bias=0.0, scale=1.0))
                    lo_eA[hf] = self.step("scalar", eA, lambda s, c=c, sl=sl, hbuf=hbuf:
                                          s.activation(
                        hbuf[:, (c + H2) * 512:(c + H2 + 1) * 512],
                        self.PS_S[:, sl * 512 + 512:sl * 512 + 1024],
                        AF.Copy, bias=0.0, scale=1.0))
                    self.ps_release(["S%d" % sl, "S%d" % (sl + 1)], eA, lo_eA[hf])
            self.wait("scalar", eV, hb_ready[NJ - 1])
            dn_pe = {}
            outs_used = {}
            xmr_used = {}
            for oc in range(DCH):
                bi = oc % 2
                if oc >= 2:
                    self.wait("gpsimd", eP, dn_pe[oc - 2])
                wdd = self.dma("gpsimd",
                               WDB[bi][:, :].rearrange("p (c o) -> p c o", o=128),
                               self.wdn[oc, :, :, :].rearrange("c p o -> p c o"), dG)
                self.wait("tensor", dG, wdd)
                for hf in range(NJ):
                    hbuf = HB if hf == 0 else HB2
                    if (oc - 1, hf) in xmr_used:
                        self.wait("sync", eV, xmr_used[(oc - 1, hf)])
                    xmd = self.dma("sync", XMR[hf % 2][:, :],
                                   self.xmid_dram[oc, :, hf * 512:(hf + 1) * 512], dS)
                    self.wait("tensor", eA, lo_eA.get(hf, 0))
                    sl = 4 + hf % 2
                    self.ps_guard("tensor", ["S%d" % sl])
                    vd = None
                    for fc in range(ICH):
                        vd = self.step("tensor", eP, lambda t, fc=fc, sl=sl, bi=bi, hbuf=hbuf:
                                       t.matmul(
                            self.PS_S[:, sl * 512:(sl + 1) * 512],
                            WDB[bi][:, fc * 128:(fc + 1) * 128],
                            hbuf[:, fc * 512:(fc + 1) * 512],
                            start=(fc == 0), stop=(fc == ICH - 1)))
                    self.wait("scalar", eP, vd)
                    if (oc - 1, hf) in xmr_used:
                        self.wait("scalar", eV, xmr_used[(oc - 1, hf)])
                    vde = self.step("scalar", eA, lambda s, sl=sl, oc=oc, hf=hf: s.activation(
                        t3p[hf % 2][:, :], self.PS_S[:, sl * 512:(sl + 1) * 512],
                        AF.Copy, bias=0.0, scale=self.dnsc[:, oc:oc + 1]))
                    self.ps_release(["S%d" % sl], eA, vde)
                    self.wait("vector", eA, vde)
                    self.wait("vector", dS, xmd)
                    if (oc - 1, hf) in outs_used:
                        self.wait("vector", dS, outs_used[(oc - 1, hf)])
                    vout = self.step("vector", eV, lambda v, hf=hf: v.tensor_add(
                        OUTS[hf % 2][:, :], t3p[hf % 2][:, :], XMR[hf % 2][:, :]))
                    xmr_used[(oc, hf)] = vout
                    self.wait("sync", eV, vout)
                    outs_used[(oc, hf)] = self.dma(
                        "sync", self.outy[oc * 128:(oc + 1) * 128, hf * 512:(hf + 1) * 512],
                        OUTS[hf % 2][:, :], dS)
                dn_pe[oc] = eP.n

    def emit(self):
        with self.nc.Block() as block:
            @block.sync
            def _(e):
                for fn in self.q["sync"]:
                    fn(e)

            @block.gpsimd
            def _(e):
                for fn in self.q["gpsimd"]:
                    fn(e)

            @block.tensor
            def _(e):
                for fn in self.q["tensor"]:
                    fn(e)

            @block.scalar
            def _(e):
                for fn in self.q["scalar"]:
                    fn(e)

            @block.vector
            def _(e):
                for fn in self.q["vector"]:
                    fn(e)



def pack_inputs(cfg, x, Wq, Wk, Wv, Wo, SU_ug, SU_dp, Wscale_ug, Wscale_dp,
                W_up, W_gate, W_down):
    DM, TT, IN = cfg.DM, cfg.TT, cfg.IN
    DCH, ICH, Q = cfg.DCH, cfg.ICH, cfg.Q
    wo_p = np.ascontiguousarray(
        Wo.reshape(DCH, 128, DCH, 128).transpose(2, 0, 1, 3))
    wup_q = W_up.T.reshape(DCH, 128, ICH, 128).transpose(2, 0, 1, 3)
    wgt_q = W_gate.T.reshape(DCH, 128, ICH, 128).transpose(2, 0, 1, 3)
    wug_p = np.ascontiguousarray(np.stack([wup_q, wgt_q], axis=1))
    wdn_p = np.ascontiguousarray(
        W_down.T.reshape(ICH, 128, DCH, 128).transpose(2, 0, 1, 3))
    su_p = np.ascontiguousarray(SU_ug.reshape(DCH, 128).T)
    upsc = np.ascontiguousarray(
        (Wscale_ug[:IN] * 64.0 * cfg.s1 * SU_dp).reshape(ICH, 128).T)
    gtsc = np.ascontiguousarray(
        (Wscale_ug[IN:] * 64.0 * cfg.s1).reshape(ICH, 128).T)
    dnsc = np.ascontiguousarray(
        (Wscale_dp * 64.0 * cfg.s2).reshape(DCH, 128).T)
    wq_c = np.ascontiguousarray(Wq)
    wk_c = np.ascontiguousarray(Wk)
    wv_c = np.ascontiguousarray(Wv)

    in_maps = []
    for core in range(8):
        b, qid = core // 4, core % 4
        perm = np.concatenate([
            np.arange(qid * Q, (qid + 1) * Q),
            np.arange(0, qid * Q),
            np.arange((qid + 1) * Q, TT)])
        xt_c = np.ascontiguousarray(x[b][perm].T)
        kval = np.zeros((1, TT), np.float32)
        kval[0, :Q + qid * Q] = 1.0
        in_maps.append(dict(
            xt=xt_c, wq=wq_c, wk=wk_c, wv=wv_c, wo=wo_p,
            wug=wug_p, wdn=wdn_p, suug=su_p, kvalid=kval,
            upsc_in=upsc, gtsc_in=gtsc, dnsc_in=dnsc))
    return in_maps


_PROGRAM_CACHE = {}


def get_program(cfg):
    key = (cfg.DM, cfg.TT, cfg.IN)
    if key not in _PROGRAM_CACHE:
        _PROGRAM_CACHE[key] = B(cfg).build()
    return _PROGRAM_CACHE[key]


def kernel(L0_attn_x, L0_attn_Wq, L0_attn_Wk, L0_attn_Wv, L0_attn_Wo,
           L0_ffn_SU_ug, L0_ffn_SU_dp, L0_ffn_Wscale_ug, L0_ffn_Wscale_dp,
           L0_ffn_W_up, L0_ffn_W_gate, L0_ffn_W_down):
    cfg = Cfg(DM=2048, TT=4096, IN=8192)
    x = np.asarray(L0_attn_x, np.float32)
    in_maps = pack_inputs(
        cfg, x, np.asarray(L0_attn_Wq, np.float32), np.asarray(L0_attn_Wk, np.float32),
        np.asarray(L0_attn_Wv, np.float32), np.asarray(L0_attn_Wo, np.float32),
        np.asarray(L0_ffn_SU_ug, np.float32), np.asarray(L0_ffn_SU_dp, np.float32),
        np.asarray(L0_ffn_Wscale_ug, np.float32), np.asarray(L0_ffn_Wscale_dp, np.float32),
        np.asarray(L0_ffn_W_up, np.float32), np.asarray(L0_ffn_W_gate, np.float32),
        np.asarray(L0_ffn_W_down, np.float32))
    nc = get_program(cfg)
    res = run_bass_kernel_spmd(nc, in_maps, core_ids=list(range(8)))
    out = np.empty((2, cfg.TT, cfg.DM), np.float32)
    for core in range(8):
        b, qid = core // 4, core % 4
        out[b, qid * cfg.Q:(qid + 1) * cfg.Q, :] = res.results[core]["outy"].T
    return out


# version 24
# speedup vs baseline: 1.1766x; 1.1766x over previous
import contextlib
import math
import numpy as np

import concourse.bass as bass
import concourse.mybir as mybir
from concourse.bass_utils import run_bass_kernel_spmd

F32 = mybir.dt.float32
BF16 = mybir.dt.bfloat16
AF = mybir.ActivationFunctionType
ALU = mybir.AluOpType

ENGINES = ("sync", "gpsimd", "tensor", "scalar", "vector")


class Cfg:
    def __init__(self, DM=2048, TT=4096, IN=8192, eps=1e-5, C=2.0, debug=False, debug2=False):
        self.DM, self.TT, self.IN, self.eps, self.C = DM, TT, IN, eps, C
        self.debug = debug
        self.debug2 = debug2
        self.DCH = DM // 128
        self.ICH = IN // 128
        self.NH = DM // 128
        self.NG = self.NH // 4
        self.Q = TT // 4
        self.NJ = self.Q // 512
        self.SCH = TT // 128
        self.NQC = self.Q // 128
        self.NT = TT // 512
        self.att_scale = TT ** -0.5
        self.s1 = 1.0 / (64.0 * math.sqrt(DM))
        self.s2 = 1.0 / (64.0 * math.sqrt(IN))
        self.expC = math.exp(-C)
        assert self.NJ >= 1 and self.NH % 4 == 0 and self.Q % 512 == 0


def mask_kind(cfg, j, c):
    if c < cfg.NQC:
        rel = c - 4 * j
        if rel < 0:
            return "ones"
        if rel < 4:
            return str(rel)
        return "zero"
    return "ones"


class Sem:
    def __init__(self, handle):
        self.h = handle
        self.n = 0


class B:

    def __init__(self, cfg):
        self.cfg = cfg
        self.nc = bass.Bass("TRN2", target_bir_lowering=False,
                            detect_race_conditions=False)
        self.q = {e: [] for e in ENGINES}
        self._last_wait = {}
        self.ctx = contextlib.ExitStack()
        self.ps_last = {}

    def on(self, eng, fn):
        self.q[eng].append(fn)

    def wait(self, eng, sem, val=None):
        if val is None:
            val = sem.n
        if val <= 0:
            return
        key = (eng, id(sem))
        if self._last_wait.get(key, -1) >= val:
            return
        self._last_wait[key] = val
        self.on(eng, lambda e, h=sem.h, v=val: e.wait_ge(h, v))

    def dma(self, queue, dst, src, dsem):
        dsem.n += 16
        self.on(queue, lambda e, a=dst, b=src, h=dsem.h: e.dma_start(out=a, in_=b).then_inc(h, 16))
        return dsem.n

    def step(self, eng, esem, fn):
        esem.n += 1
        self.on(eng, lambda e, f=fn, h=esem.h: f(e).then_inc(h, 1))
        return esem.n

    def quiet(self, eng, fn):
        self.on(eng, lambda e, f=fn: f(e))

    def sbuf(self, name, shape, dtype, stack=None):
        return (stack or self.ctx).enter_context(self.nc.sbuf_tensor(name, shape, dtype))

    def psum(self, name, shape, dtype=F32):
        return self.ctx.enter_context(self.nc.psum_tensor(name, shape, dtype))

    def sem(self, name):
        return Sem(self.ctx.enter_context(self.nc.semaphore(name)))

    def ps_guard(self, eng, banks):
        for bk in banks:
            if bk in self.ps_last:
                s, v = self.ps_last[bk]
                self.wait(eng, s, v)

    def ps_release(self, banks, sem, val):
        for bk in banks:
            self.ps_last[bk] = (sem, val)

    def barrier(self):
        for e in ENGINES:
            for s in self.all_sems:
                self.wait(e, s)

    def build(self):
        cfg = self.cfg
        nc = self.nc
        DM, TT, IN = cfg.DM, cfg.TT, cfg.IN
        DCH, ICH, NG, Q, NJ = cfg.DCH, cfg.ICH, cfg.NG, cfg.Q, cfg.NJ
        SCH, NT = cfg.SCH, cfg.NT

        dt = nc.dram_tensor
        self.xt = dt("xt", [DM, TT], F32, kind="ExternalInput").ap()
        self.wq = dt("wq", [DM, DM], F32, kind="ExternalInput").ap()
        self.wk = dt("wk", [DM, DM], F32, kind="ExternalInput").ap()
        self.wv = dt("wv", [DM, DM], F32, kind="ExternalInput").ap()
        self.wo = dt("wo", [DCH, DCH, 128, 128], F32, kind="ExternalInput").ap()
        self.wug = dt("wug", [ICH, 2, DCH, 128, 128], F32, kind="ExternalInput").ap()
        self.wdn = dt("wdn", [DCH, ICH, 128, 128], F32, kind="ExternalInput").ap()
        self.suug = dt("suug", [128, DM // 128], F32, kind="ExternalInput").ap()
        self.kvalid = dt("kvalid", [1, TT], F32, kind="ExternalInput").ap()
        self.upsc_in = dt("upsc_in", [128, ICH], F32, kind="ExternalInput").ap()
        self.gtsc_in = dt("gtsc_in", [128, ICH], F32, kind="ExternalInput").ap()
        self.dnsc_in = dt("dnsc_in", [128, DCH], F32, kind="ExternalInput").ap()
        self.outy = dt("outy", [DM, Q], F32, kind="ExternalOutput").ap()
        kk = "ExternalOutput" if cfg.debug2 else "Internal"
        self.ao_dram = dt("ao_dram", [DCH, 128, Q], BF16, kind=kk).ap()
        if cfg.debug:
            self.dbg1 = dt("dbg1", [128, ICH * 512], BF16, kind="Internal").ap()
            self.dbg2 = dt("dbg2", [128, ICH * 512], BF16, kind="Internal").ap()
        self.xmid_dram = dt("xmid_dram", [DCH, 128, Q], F32, kind=kk).ap()
        if cfg.debug2:
            self.ao2_out = dt("ao2_out", [DCH, 128, Q], BF16, kind="ExternalOutput").ap()
            self.hsu_out = dt("hsu_out", [128, DCH], F32, kind="ExternalOutput").ap()
            self.x1_out = dt("x1_out", [128, DCH * Q], BF16, kind="ExternalOutput").ap()
            self.hb_out = dt("hb_out", [128, (IN // 128) * 512], BF16, kind="ExternalOutput").ap()

        with self.ctx:
            self.dG = self.sem("dG")
            self.dS = self.sem("dS")
            self.eP = self.sem("eP")
            self.eA = self.sem("eA")
            self.eV = self.sem("eV")
            self.eG = self.sem("eG")
            self.all_sems = [self.dG, self.dS, self.eP, self.eA, self.eV, self.eG]

            self.ones_c = self.sbuf("ones_c", [128, 1], BF16)
            self.ones_cf = self.sbuf("ones_cf", [128, 1], F32)
            self.ones_r = self.sbuf("ones_r", [1, 128], F32)
            self.one11 = self.sbuf("one11", [1, 1], F32)
            self.hmat = self.sbuf("hmat", [128, 128], BF16)
            self.hneg = self.sbuf("hneg", [128, 128], BF16)
            self.upsc = self.sbuf("upsc", [128, ICH], F32)
            self.gtsc = self.sbuf("gtsc", [128, ICH], F32)
            self.dnsc = self.sbuf("dnsc", [128, DCH], F32)
            self.negC = self.sbuf("negC", [128, 1], F32)
            self.epsc = self.sbuf("epsc", [1, 1], F32)

            self.PS_S = self.psum("PS_S", [128, 3072], F32)
            self.PS_AV = self.psum("PS_AV", [128, 1024], F32)

            self.setup()
            self.phase_attention()
            self.barrier()
            if cfg.debug2:
                self.dma("sync", self.ao2_out, self.ao_dram[:, :, :], self.dS)
                self.wait("gpsimd", self.dS)
            self.HB = self.sbuf("HB", [128, (IN // 128) * 512], BF16)
            self.X1 = self.sbuf("X1", [128, DCH * Q], BF16)
            self.phase_wo_ln2()
            self.phase_ffn()
            self.wait("sync", self.dS)
            self.emit()
        return self.nc

    def setup(self):
        cfg = self.cfg
        self.quiet("vector", lambda v: v.memset(self.ones_c[:, :], 1.0))
        self.quiet("vector", lambda v: v.memset(self.ones_cf[:, :], 1.0))
        self.quiet("vector", lambda v: v.memset(self.ones_r[:, :], 1.0))
        self.quiet("vector", lambda v: v.memset(self.one11[:, :], 1.0))
        self.quiet("vector", lambda v: v.memset(self.negC[:, :], -self.cfg.C))
        self.quiet("vector", lambda v: v.memset(self.epsc[:, :], self.cfg.eps))
        self.quiet("gpsimd", lambda g: g.memset(self.hmat[:, :], 0.0))
        self.quiet("gpsimd", lambda g: g.memset(self.hneg[:, :], 0.0))
        self.quiet("gpsimd", lambda g: g.memset(self.hmat[0:1, 0:1], 1.0))
        self.quiet("gpsimd", lambda g: g.memset(self.hneg[0:1, 0:1], -1.0))
        hs = self.dG
        for n in (1, 2, 4, 8, 16, 32, 64):
            self.dma("gpsimd", self.hmat[0:n, n:2 * n], self.hmat[0:n, 0:n], hs)
            self.dma("gpsimd", self.hmat[n:2 * n, 0:n], self.hmat[0:n, 0:n], hs)
            self.dma("gpsimd", self.hmat[n:2 * n, n:2 * n], self.hneg[0:n, 0:n], hs)
            self.wait("gpsimd", hs)
            self.dma("gpsimd", self.hneg[0:2 * n, 0:2 * n], self.hmat[0:2 * n, 0:2 * n], hs)
            self.wait("gpsimd", hs)
            self.eG.n += 1
            self.on("gpsimd", lambda g, n=n, h=self.eG.h: g.tensor_scalar_mul(
                self.hneg[0:2 * n, 0:2 * n], self.hneg[0:2 * n, 0:2 * n], -1.0).then_inc(h, 1))
            self.wait("gpsimd", self.eG)
        self.dma("gpsimd", self.upsc[:, :], self.upsc_in[:, :], self.dG)
        self.dma("gpsimd", self.gtsc[:, :], self.gtsc_in[:, :], self.dG)
        self.dma("gpsimd", self.dnsc[:, :], self.dnsc_in[:, :], self.dG)
        self.setup_dG = self.dG.n
        self.setup_eG = self.eG.n

    def bcast_row(self, row_ap, ps_ap, banks):
        self.ps_guard("tensor", banks)
        return self.step("tensor", self.eP, lambda t, r=row_ap, o=ps_ap:
                         t.matmul(o, self.ones_r[:, :], r, start=True, stop=True))

    def row_to_col(self, row_slice, ps_ap, banks):
        self.ps_guard("tensor", banks)
        return self.step("tensor", self.eP, lambda t, r=row_slice, o=ps_ap:
                         t.matmul(o, r, self.one11[:, :], start=True, stop=True))

    def phase_attention(self):
        cfg = self.cfg
        DM, TT, Q, DCH, NG, NJ, SCH, NT = (cfg.DM, cfg.TT, cfg.Q, cfg.DCH,
                                           cfg.NG, cfg.NJ, cfg.SCH, cfg.NT)
        pha = contextlib.ExitStack()
        sb = lambda n, s, d: self.sbuf(n, s, d, pha)
        with pha:
            XT = [sb("XT0", [128, DCH * 512], BF16), sb("XT1", [128, DCH * 512], BF16)]
            self.masks = sb("masks", [128, 4 * 512], BF16)
            self.kval_r = sb("kval_r", [1, TT], BF16)
            self.quiet("gpsimd", lambda g: g.memset(self.masks[:, :], 1.0))
            for r in range(4):
                self.quiet("gpsimd", lambda g, r=r: g.affine_select(
                    out=self.masks[:, r * 512:(r + 1) * 512],
                    in_=self.masks[:, r * 512:(r + 1) * 512],
                    compare_op=ALU.is_ge, fill=0.0, base=-128 * r,
                    pattern=[[1, 512]], channel_multiplier=-1))
            self.eG.n += 1
            self.on("gpsimd", lambda g, h=self.eG.h: g.engine_nop().then_inc(h, 1))
            self.setup_eG = self.eG.n
            self.dma("gpsimd", self.kval_r[:, :], self.kvalid[:, :], self.dG)
            self.setup_dG = self.dG.n
            KG = sb("KG", [128, 4 * TT], BF16)
            VG = sb("VG", [128, SCH * 512], BF16)
            QG = sb("QG", [128, 4 * Q], BF16)
            APB = sb("APB", [128, TT], BF16)
            BPB = sb("BPB", [128, TT], BF16)
            PT = [sb("PT0", [128, 1024], BF16), sb("PT1", [128, 1024], BF16),
                  sb("PT2", [128, 1024], BF16)]
            DACC = [sb("DACC0", [128, 512], F32), sb("DACC1", [128, 512], F32)]
            SMB = sb("SMB", [128, 512], BF16)
            WKB = sb("WKB", [128, DCH * 512], BF16)
            WVB = sb("WVB", [128, DCH * 512], BF16)
            WQB = sb("WQB", [128, DCH * 512], BF16)
            rowsA = sb("rowsA", [1, 1536], F32)
            colk = sb("colk", [128, 8], F32)
            colvb = sb("colvb", [128, 512], BF16)
            rdenb = [sb("rden0", [128, 512], BF16), sb("rden1", [128, 512], BF16)]
            DTS = [sb("DTS0", [128, 512], BF16), colvb]
            t1s = sb("t1s", [128, 512], F32)
            aost = [sb("aost0", [128, 512], BF16), sb("aost1", [128, 512], BF16)]
            abcols = sb("abcols", [128, 2 * SCH], F32)
            SQ = WQB

            eP, eA, eV, eG, dG, dS = self.eP, self.eA, self.eV, self.eG, self.dG, self.dS
            ln_eV = {}
            alcol_eV = {}
            fix_eV = {}
            x_dma = {}
            xsrc = self.xt.rearrange("(c p) (n t) -> p c n t", p=128, t=512)

            def load_x(tt, bi):
                x_dma[(tt, bi)] = self.dma(
                    "gpsimd", XT[bi][:, :].rearrange("p (c t) -> p c t", t=512),
                    xsrc[:, :, tt, :], dG)

            def load_w(dram, buf):
                src = dram.rearrange("(c p) o -> p c o", p=128)
                return self.dma("gpsimd", buf[:, :].rearrange("p (c o) -> p c o", o=512),
                                src[:, :, :], dG)

            def emit_stats(tt, xb, xdma):
                self.wait("gpsimd", dG, xdma)
                vg = self.eG.n + 1
                self.eG.n = vg
                self.on("gpsimd", lambda g, x=xb, h=eG.h: g.tensor_tensor(
                    SQ[:, :], x[:, :], x[:, :], ALU.mult).then_inc(h, 1))
                self.wait("tensor", dG, xdma)
                self.ps_guard("tensor", ["S4", "S5"])
                for c in range(DCH):
                    self.quiet("tensor", lambda t, c=c, x=xb: t.matmul(
                        self.PS_S[0:1, 2048:2560], self.ones_c[:, :], x[:, c * 512:(c + 1) * 512],
                        start=(c == 0), stop=(c == DCH - 1)))
                self.wait("tensor", eG, vg)
                v2 = None
                for c in range(DCH):
                    v2 = self.step("tensor", eP, lambda t, c=c: t.matmul(
                        self.PS_S[0:1, 2560:3072], self.ones_c[:, :], SQ[:, c * 512:(c + 1) * 512],
                        start=(c == 0), stop=(c == DCH - 1)))
                self.wait("vector", eP, v2)
                self.quiet("vector", lambda v: v.tensor_scalar_mul(
                    rowsA[:, 0:512], self.PS_S[0:1, 2048:2560], 1.0 / DM))
                self.quiet("vector", lambda v: v.tensor_scalar_mul(
                    rowsA[:, 512:1024], self.PS_S[0:1, 2560:3072], 1.0 / DM))
                self.quiet("vector", lambda v: v.tensor_tensor(
                    t1s[0:1, 0:512], rowsA[:, 0:512], rowsA[:, 0:512], ALU.mult))
                vv = self.step("vector", eV, lambda v: v.tensor_sub(
                    rowsA[:, 512:1024], rowsA[:, 512:1024], t1s[0:1, 0:512]))
                self.ps_release(["S4", "S5"], eV, vv)
                self.wait("scalar", eV, vv)
                self.quiet("scalar", lambda s: s.activation(
                    rowsA[:, 512:1024], rowsA[:, 512:1024], AF.Ln, bias=self.epsc[0:1, 0:1], scale=1.0))
                va = self.step("scalar", eA, lambda s: s.activation(
                    rowsA[:, 512:1024], rowsA[:, 512:1024], AF.Exp, bias=0.0, scale=-0.5))
                self.wait("vector", eA, va)
                self.wait("vector", dG, self.setup_dG)
                self.quiet("vector", lambda v: v.scalar_tensor_tensor(
                    t1s[0:1, 0:512], rowsA[:, 0:512], -1.0, rowsA[:, 512:1024],
                    op0=ALU.mult, op1=ALU.mult))
                self.quiet("vector", lambda v, tt=tt: v.tensor_tensor(
                    rowsA[:, 0:512], rowsA[:, 512:1024],
                    self.kval_r[:, tt * 512:(tt + 1) * 512], ALU.mult))
                vb = self.step("vector", eV, lambda v, tt=tt: v.tensor_tensor(
                    rowsA[:, 1024:1536], t1s[0:1, 0:512],
                    self.kval_r[:, tt * 512:(tt + 1) * 512], ALU.mult))
                self.wait("tensor", eA, va)
                self.wait("tensor", eV, vb)
                for sub in range(4):
                    scn = tt * 4 + sub
                    v1 = self.row_to_col(rowsA[:, 512 + sub * 128:512 + (sub + 1) * 128],
                                         self.PS_AV[:, 0:1], ["AV0"])
                    v1 = self.row_to_col(t1s[0:1, sub * 128:(sub + 1) * 128],
                                         self.PS_AV[:, 1:2], ["AV0"])
                    self.wait("vector", eP, v1)
                    alcol_eV[scn] = self.step("vector", eV, lambda v, scn=scn: v.tensor_copy(
                        abcols[:, 2 * scn:2 * scn + 2], self.PS_AV[:, 0:2]))
                    self.ps_release(["AV0"], eV, alcol_eV[scn])
                self.wait("tensor", eV, vb)
                p1 = self.bcast_row(rowsA[:, 0:512], self.PS_S[:, 0:512], ["S0"])
                self.wait("vector", eP, p1)
                c1 = self.step("vector", eV, lambda v, tt=tt: v.tensor_copy(
                    APB[:, tt * 512:(tt + 1) * 512], self.PS_S[:, 0:512]))
                self.ps_release(["S0"], eV, c1)
                p2 = self.bcast_row(rowsA[:, 1024:1536], self.PS_S[:, 512:1024], ["S1"])
                self.wait("vector", eP, p2)
                c2 = self.step("vector", eV, lambda v, tt=tt: v.tensor_copy(
                    BPB[:, tt * 512:(tt + 1) * 512], self.PS_S[:, 512:1024]))
                self.ps_release(["S1"], eV, c2)
                ln_eV[tt] = max(c1, c2)

            def colsum_cols(wbuf, dst_off):
                for dkc in range(4):
                    self.ps_guard("tensor", ["S0"])
                    vcs = None
                    for c in range(DCH):
                        vcs = self.step("tensor", eP, lambda t, c=c, dkc=dkc: t.matmul(
                            self.PS_S[:, 0:1],
                            wbuf[:, c * 512 + dkc * 128:c * 512 + (dkc + 1) * 128],
                            self.ones_c[:, :], start=(c == 0), stop=(c == DCH - 1)))
                    self.wait("vector", eP, vcs)
                    vc = self.step("vector", eV, lambda v, dkc=dkc, o=dst_off: v.tensor_copy(
                        colk[:, o + dkc:o + dkc + 1], self.PS_S[:, 0:1]))
                    self.ps_release(["S0"], eV, vc)

            def proj_fix_feature_major(slot_banks, psum_slice, dst, tt_or_j, colcol):
                self.wait("vector", eV, ln_eV[tt_or_j])
                self.quiet("vector", lambda v, ps=psum_slice, tt=tt_or_j: v.tensor_tensor(
                    t1s[:, :], ps, APB[:, tt * 512:(tt + 1) * 512], ALU.mult))
                vfix = self.step("vector", eV, lambda v, d=dst, tt=tt_or_j, cc=colcol:
                                 v.scalar_tensor_tensor(
                                     d, BPB[:, tt * 512:(tt + 1) * 512], cc,
                                     t1s[:, :], op0=ALU.mult, op1=ALU.add))
                self.ps_release(slot_banks, eV, vfix)
                return vfix

            wkd = wvd = wqd = 0
            kvq_done = [0] * NG
            SBANK = ["S0", "S1", "S2", "S3", "S4", "S5"]
            hj = 0
            sfree_a = {}
            sfree_v = {}
            pfree = {}
            pt_last = {}
            pt_last_v = {}
            smb_last = {}
            ao_store = {}

            def emit_av_denom(hjk, b, hl, j, par, first, exp_a, exp_v):
                nb = SCH // 2
                last = (b == nb - 1)
                slot = b % 3
                self.wait("tensor", eA, exp_a)
                if exp_v:
                    self.wait("tensor", eV, exp_v)
                if first:
                    self.ps_guard("tensor", ["AV%d" % par, "D%d" % par])
                vav = None
                for ci, c in enumerate((2 * b, 2 * b + 1)):
                    vav = self.step("tensor", eP, lambda t, c=c, ci=ci, hl=hl, slot=slot, par=par:
                                    t.matmul(
                        self.PS_AV[:, par * 512:(par + 1) * 512],
                        VG[:, c * 512 + hl * 128:c * 512 + (hl + 1) * 128],
                        PT[slot][:, ci * 512:(ci + 1) * 512],
                        start=(first and ci == 0), stop=(last and ci == 1)))
                self.wait("vector", eA, exp_a)
                if exp_v:
                    self.wait("vector", eV, exp_v)
                ts = b % 2
                vp = self.step("vector", eV, lambda v, slot=slot, ts=ts: v.tensor_add(
                    DTS[ts][:, :], PT[slot][:, 0:512], PT[slot][:, 512:1024]))
                vdn = vp
                if ts == 1:
                    self.quiet("vector", lambda v: v.tensor_add(
                        DTS[0][:, :], DTS[0][:, :], DTS[1][:, :]))
                    if b == 1:
                        self.wait("vector", eP, self.dacc_guard.get(par, 0))
                        vdn = self.step("vector", eV, lambda v, par=par: v.tensor_copy(
                            DACC[par][:, :], DTS[0][:, :]))
                    else:
                        vdn = self.step("vector", eV, lambda v, par=par: v.tensor_add(
                            DACC[par][:, :], DACC[par][:, :], DTS[0][:, :]))
                pfree[(hjk, b)] = max(vav, vdn)
                pt_last[slot] = vav
                pt_last_v[slot] = vp
                return vav, vdn

            self.pending_finish = None
            self.dacc_guard = {}

            def finish_hj(par, g, hl, j, vav, vdn):
                self.ps_guard("tensor", ["S5"])
                self.wait("tensor", eV, vdn)
                vsum = self.step("tensor", eP, lambda t, par=par: t.matmul(
                    self.PS_S[0:1, 2560:3072],
                    self.ones_cf[:, :], DACC[par][:, :], start=True, stop=True))
                self.wait("scalar", eP, vsum)
                self.quiet("scalar", lambda s: s.activation(
                    rowsA[0:1, 0:512], self.PS_S[0:1, 2560:3072],
                    AF.Ln, bias=0.0, scale=1.0))
                rrec = self.step("scalar", eA, lambda s: s.activation(
                    rowsA[0:1, 0:512], rowsA[0:1, 0:512],
                    AF.Exp, bias=0.0, scale=-1.0))
                self.wait("tensor", eA, rrec)
                pbc = self.bcast_row(rowsA[0:1, 0:512],
                                     self.PS_S[:, 2560:3072], [])
                self.wait("vector", eP, pbc)
                vrd = self.step("vector", eV, lambda v, par=par: v.tensor_copy(
                    rdenb[par][:, :], self.PS_S[:, 2560:3072]))
                self.ps_release(["S5"], eV, vrd)
                self.wait("vector", eP, vav)
                if par in ao_store:
                    self.wait("vector", dS, ao_store[par])
                vev = self.step("vector", eV, lambda v, par=par: v.tensor_mul(
                    aost[par][:, :], self.PS_AV[:, par * 512:(par + 1) * 512],
                    rdenb[par][:, :]))
                self.ps_release(["AV%d" % par], eV, vev)
                self.wait("sync", eV, vev)
                ch = g * 4 + hl
                ao_store[par] = self.dma(
                    "sync", self.ao_dram[ch, :, j * 512:(j + 1) * 512],
                    aost[par][:, :], dS)
                self.dacc_guard[par] = vsum

            for g in range(NG):
                if g > 0:
                    self.wait("gpsimd", eP, kvq_done[g - 1])
                wkd = load_w(self.wk[:, g * 512:(g + 1) * 512], WKB)
                wvd = load_w(self.wv[:, g * 512:(g + 1) * 512], WVB)
                self.wait("tensor", dG, wkd)
                colsum_cols(WKB, 0)
                self.wait("tensor", dG, wvd)
                self.ps_guard("tensor", ["S1"])
                vcs = None
                for c in range(DCH):
                    vcs = self.step("tensor", eP, lambda t, c=c: t.matmul(
                        self.PS_S[0:1, 512:1024], self.ones_c[:, :], WVB[:, c * 512:(c + 1) * 512],
                        start=(c == 0), stop=(c == DCH - 1)))
                self.wait("vector", eP, vcs)
                vr = self.step("vector", eV, lambda v: v.tensor_copy(
                    t1s[0:1, 0:512], self.PS_S[0:1, 512:1024]))
                self.ps_release(["S1"], eV, vr)
                self.wait("tensor", eV, vr)
                pb = self.bcast_row(t1s[0:1, 0:512], self.PS_S[:, 0:512], ["S0"])
                self.wait("vector", eP, pb)
                cvb_eV = self.step("vector", eV, lambda v: v.tensor_copy(
                    colvb[:, :], self.PS_S[:, 0:512]))
                self.ps_release(["S0"], eV, cvb_eV)

                x_used = {}
                for pos in range(NT):
                    tt = pos
                    bi = pos % 2
                    if pos >= 2:
                        self.wait("gpsimd", eP, x_used[pos - 2])
                    load_x(tt, bi)
                    xb = XT[bi]
                    if g == 0:
                        emit_stats(tt, xb, x_dma[(tt, bi)])
                    self.wait("tensor", dG, max(x_dma[(tt, bi)], wkd))
                    for dkc in range(4):
                        sl = dkc % 2
                        self.ps_guard("tensor", [SBANK[sl]])
                        vl = None
                        for c in range(DCH):
                            vl = self.step("tensor", eP, lambda t, c=c, dkc=dkc, sl=sl, xb=xb:
                                           t.matmul(
                                self.PS_S[:, sl * 512:(sl + 1) * 512],
                                WKB[:, c * 512 + dkc * 128:c * 512 + (dkc + 1) * 128],
                                xb[:, c * 512:(c + 1) * 512],
                                start=(c == 0), stop=(c == DCH - 1)))
                        self.wait("vector", eP, vl)
                        proj_fix_feature_major(
                            [SBANK[sl]], self.PS_S[:, sl * 512:(sl + 1) * 512],
                            KG[:, dkc * TT + tt * 512:dkc * TT + (tt + 1) * 512],
                            tt, colk[:, dkc:dkc + 1])
                    self.wait("tensor", dG, wvd)
                    self.wait("tensor", eV, cvb_eV)
                    for sub in range(4):
                        scn = tt * 4 + sub
                        sl = 2 + sub % 2
                        self.ps_guard("tensor", [SBANK[sl]])
                        vl = None
                        for c in range(DCH):
                            vl = self.step("tensor", eP, lambda t, c=c, sub=sub, sl=sl, xb=xb:
                                           t.matmul(
                                self.PS_S[:, sl * 512:(sl + 1) * 512],
                                xb[:, c * 512 + sub * 128:c * 512 + (sub + 1) * 128],
                                WVB[:, c * 512:(c + 1) * 512],
                                start=(c == 0), stop=(c == DCH - 1)))
                        self.wait("vector", eP, vl)
                        self.wait("vector", eV, alcol_eV[scn])
                        self.quiet("vector", lambda v, scn=scn, sl=sl: v.tensor_scalar(
                            t1s[:, :], self.PS_S[:, sl * 512:(sl + 1) * 512],
                            abcols[:, 2 * scn:2 * scn + 1], None, op0=ALU.mult))
                        vfx = self.step("vector", eV, lambda v, scn=scn: v.scalar_tensor_tensor(
                            VG[:, scn * 512:(scn + 1) * 512],
                            colvb[:, :], abcols[:, 2 * scn + 1:2 * scn + 2],
                            t1s[:, :], op0=ALU.mult, op1=ALU.add))
                        self.ps_release([SBANK[sl]], eV, vfx)
                    x_used[pos] = eP.n

                self.wait("gpsimd", eP, eP.n)
                wqd = load_w(self.wq[:, g * 512:(g + 1) * 512], WQB)
                for j in range(NJ):
                    load_x(j, j % 2)
                self.wait("tensor", dG, wqd)
                colsum_cols(WQB, 4)
                for j in range(NJ):
                    xb = XT[j % 2]
                    self.wait("tensor", dG, x_dma[(j, j % 2)])
                    for dkc in range(4):
                        sl = dkc % 2
                        self.ps_guard("tensor", [SBANK[sl]])
                        vl = None
                        for c in range(DCH):
                            vl = self.step("tensor", eP, lambda t, c=c, dkc=dkc, sl=sl, xb=xb:
                                           t.matmul(
                                self.PS_S[:, sl * 512:(sl + 1) * 512],
                                WQB[:, c * 512 + dkc * 128:c * 512 + (dkc + 1) * 128],
                                xb[:, c * 512:(c + 1) * 512],
                                start=(c == 0), stop=(c == DCH - 1)))
                        self.wait("vector", eP, vl)
                        proj_fix_feature_major(
                            [SBANK[sl]], self.PS_S[:, sl * 512:(sl + 1) * 512],
                            QG[:, dkc * Q + j * 512:dkc * Q + (j + 1) * 512],
                            j, colk[:, 4 + dkc:5 + dkc])
                kvq_done[g] = eP.n
                qfix_eV = eV.n

                for hl in range(4):
                    for j in range(NJ):
                        par = hj % 2
                        nb = SCH // 2
                        self.wait("tensor", eV, qfix_eV)
                        prev = None
                        first = True
                        for b in range(nb):
                            if b == 3 and self.pending_finish is not None:
                                self.pending_finish()
                                self.pending_finish = None
                            c0, c1 = 2 * b, 2 * b + 1
                            kinds = (mask_kind(cfg, j, c0), mask_kind(cfg, j, c1))
                            slot = b % 3
                            banks = ["S%d" % (2 * slot), "S%d" % (2 * slot + 1)]
                            self.ps_guard("tensor", banks)
                            if slot in pt_last:
                                self.wait("scalar", eP, pt_last[slot])
                                self.wait("vector", eP, pt_last[slot])
                                self.wait("scalar", eV, pt_last_v[slot])
                                self.wait("vector", eV, pt_last_v[slot])
                            smm = None
                            for ci, c in enumerate((c0, c1)):
                                if kinds[ci] == "zero":
                                    continue
                                smm = self.step("tensor", eP, lambda t, hl=hl, j=j, c=c,
                                                slot=slot, ci=ci: t.matmul(
                                    self.PS_S[:, slot * 1024 + ci * 512:slot * 1024 + (ci + 1) * 512],
                                    KG[:, hl * TT + c * 128:hl * TT + (c + 1) * 128],
                                    QG[:, hl * Q + j * 512:hl * Q + (j + 1) * 512],
                                    start=True, stop=True))
                            ea = ev = 0
                            if kinds == ("ones", "ones"):
                                self.wait("scalar", eP, smm)
                                ea = self.step("scalar", eA, lambda s, slot=slot: s.activation(
                                    PT[slot][:, :], self.PS_S[:, slot * 1024:(slot + 1) * 1024],
                                    AF.Exp, bias=self.negC[:, 0:1], scale=cfg.att_scale))
                                self.ps_release(banks, eA, ea)
                            else:
                                for ci, c in enumerate((c0, c1)):
                                    kk = kinds[ci]
                                    if kk == "zero":
                                        ev = self.step("vector", eV, lambda v, slot=slot, ci=ci:
                                                       v.memset(PT[slot][:, ci * 512:(ci + 1) * 512],
                                                                cfg.expC))
                                    elif kk == "ones":
                                        self.wait("scalar", eP, smm)
                                        ea = self.step("scalar", eA, lambda s, slot=slot, ci=ci:
                                                       s.activation(
                                            PT[slot][:, ci * 512:(ci + 1) * 512],
                                            self.PS_S[:, slot * 1024 + ci * 512:slot * 1024 + (ci + 1) * 512],
                                            AF.Exp, bias=self.negC[:, 0:1], scale=cfg.att_scale))
                                        self.ps_release(banks[ci:ci + 1], eA, ea)
                                    else:
                                        r = int(kk)
                                        self.wait("vector", eP, smm)
                                        self.wait("vector", eG, self.setup_eG)
                                        if 0 in smb_last:
                                            self.wait("vector", eA, smb_last[0])
                                        vm = self.step("vector", eV, lambda v, slot=slot, ci=ci, r=r:
                                                       v.tensor_tensor(
                                            SMB[:, 0:512],
                                            self.PS_S[:, slot * 1024 + ci * 512:slot * 1024 + (ci + 1) * 512],
                                            self.masks[:, r * 512:(r + 1) * 512], ALU.mult))
                                        self.ps_release(banks[ci:ci + 1], eV, vm)
                                        self.wait("scalar", eV, vm)
                                        ea = self.step("scalar", eA, lambda s, slot=slot, ci=ci:
                                                       s.activation(
                                            PT[slot][:, ci * 512:(ci + 1) * 512],
                                            SMB[:, 0:512],
                                            AF.Exp, bias=self.negC[:, 0:1], scale=cfg.att_scale))
                                        smb_last[0] = ea
                            sfree_a[(hj, b)] = ea
                            sfree_v[(hj, b)] = ev
                            if prev is not None:
                                emit_av_denom(hj, prev, hl, j, par, first,
                                              sfree_a[(hj, prev)], sfree_v[(hj, prev)])
                                first = False
                            prev = b
                        vav, vdn = emit_av_denom(hj, prev, hl, j, par, first,
                                                 sfree_a[(hj, prev)], sfree_v[(hj, prev)])
                        self.pending_finish = (lambda par=par, g=g, hl=hl, j=j,
                                               vav=vav, vdn=vdn:
                                               finish_hj(par, g, hl, j, vav, vdn))
                        hj += 1

                if self.pending_finish is not None:
                    self.pending_finish()
                    self.pending_finish = None
            self.wait("sync", dS)

    def emit_butterflies(self, hv, lo_chunks, hi_chunks, BFA, BFB, tq, ttot,
                         ts_list=None):
        eV = self.eV
        n = hi_chunks - lo_chunks
        nst = int(math.log2(n))
        if nst == 1:
            self.emit_cross_butterfly(hv[:, lo_chunks:hi_chunks, :], 1, 0, BFA, tq, ttot,
                                      ts_list=ts_list)
            return
        for ts in (range(ttot // tq) if ts_list is None else ts_list):
            cur = hv[:, lo_chunks:hi_chunks, ts * tq:(ts + 1) * tq]
            prevb = None
            for st in range(nst):
                src_ = cur if st == 0 else prevb
                dstbuf = (BFA if st % 2 == 0 else BFB)
                dst = dstbuf[:, 0:n * tq].rearrange("p (c q) -> p c q", q=tq)
                if st == nst - 1:
                    dst = cur
                h = 1 << st
                sv = src_.rearrange("p (bk two h) q -> p bk two h q", two=2, h=h)
                dv = dst.rearrange("p (bk two h) q -> p bk two h q", two=2, h=h)
                self.quiet("vector", lambda v, sv=sv, dv=dv: v.tensor_add(
                    dv[:, :, 0, :, :], sv[:, :, 0, :, :], sv[:, :, 1, :, :]))
                self.step("vector", eV, lambda v, sv=sv, dv=dv: v.tensor_sub(
                    dv[:, :, 1, :, :], sv[:, :, 0, :, :], sv[:, :, 1, :, :]))
                prevb = dst

    def emit_cross_butterfly(self, hv, dist, block_lo, BFS, tq, ttot, BFS2=None,
                             ts_list=None):
        eV = self.eV
        a = hv[:, block_lo:block_lo + dist, :]
        b = hv[:, block_lo + dist:block_lo + 2 * dist, :]
        sc = BFS[:, 0:dist * tq].rearrange("p (c q) -> p c q", q=tq)
        if BFS2 is None:
            BFS2 = BFS
            half = dist * tq
        else:
            half = 0
        sc2 = BFS2[:, half:half + dist * tq].rearrange("p (c q) -> p c q", q=tq) \
            if half else BFS2[:, 0:dist * tq].rearrange("p (c q) -> p c q", q=tq)
        for ts in (range(ttot // tq) if ts_list is None else ts_list):
            asl = a[:, :, ts * tq:(ts + 1) * tq]
            bsl = b[:, :, ts * tq:(ts + 1) * tq]
            self.quiet("vector", lambda v, asl=asl, bsl=bsl, sc=sc: v.tensor_sub(
                sc, asl, bsl))
            self.quiet("vector", lambda v, asl=asl, bsl=bsl, sc2=sc2: v.tensor_add(
                sc2, asl, bsl))
            self.quiet("vector", lambda v, asl=asl, sc2=sc2: v.tensor_copy(asl, sc2))
            self.step("vector", eV, lambda v, bsl=bsl, sc=sc: v.tensor_copy(bsl, sc))

    def butterfly_triggers(self, hv, done_chunks, total, QC, BFA, BFB, tq, ttot):
        if done_chunks % QC == 0:
            self.emit_butterflies(hv, done_chunks - QC, done_chunks, BFA, BFB, tq, ttot)
        d = 2 * QC
        while d <= total and done_chunks % d == 0:
            self.emit_cross_butterfly(hv, d // 2, done_chunks - d, BFA, tq, ttot)
            d *= 2


    def phase_wo_ln2(self):
        cfg = self.cfg
        DM, Q, DCH, NJ = cfg.DM, cfg.Q, cfg.DCH, cfg.NJ
        eP, eA, eV, eG, dG, dS = self.eP, self.eA, self.eV, self.eG, self.dG, self.dS
        X1 = self.X1
        QCB = max(DCH // 4, 2)
        phb = contextlib.ExitStack()
        sb = lambda n, s, d: self.sbuf(n, s, d, phb)
        with phb:
            AOR = [sb("AOR0", [128, DCH * 512], BF16), sb("AOR1", [128, DCH * 512], BF16)]
            WOB = [sb("WOB0", [128, DCH * 128], BF16), sb("WOB1", [128, DCH * 128], BF16)]
            XOW = [sb("XOW0", [128, 512], F32), sb("XOW1", [128, 512], F32)]
            XMF = [sb("XMF0", [128, 512], F32), sb("XMF1", [128, 512], F32)]
            XB = [sb("XB0", [128, 512], BF16), sb("XB1", [128, 512], BF16)]
            SQT = [sb("SQT0", [128, 512], BF16), sb("SQT1", [128, 512], BF16)]
            XN = [sb("XN0", [128, 512], BF16), sb("XN1", [128, 512], BF16)]
            t2s = sb("t2s", [1, 512], F32)
            t2f = sb("t2f", [128, 512], F32)
            self.rows2 = self.sbuf("rows2", [1, 2 * cfg.Q], F32, phb)
            sucol = sb("sucol", [128, DM // 128], BF16)
            sucolf = sb("sucolf", [128, DM // 128], F32)
            HSU = sb("HSU", [128, DM // 128], F32)
            A2B = sb("A2B", [128, 512], F32)
            B2B = sb("B2B", [128, 512], F32)
            BFA = sb("BFAw", [128, DCH * 128], BF16)
            BFB = sb("BFBw", [128, max(QCB, 2) * 128], BF16)

            sud = self.dma("gpsimd", sucol[:, :], self.suug[:, :], dG)
            self.dma("gpsimd", sucolf[:, :], self.suug[:, :], dG)
            sud = dG.n
            self.wait("tensor", dG, sud)
            self.wait("tensor", eG, self.setup_eG)
            self.ps_guard("tensor", ["S2"])
            vhs = None
            for cp in range(DCH):
                for c in range(DCH):
                    neg = bin(c & cp).count("1") % 2 == 1
                    mat = self.hneg if neg else self.hmat
                    vhs = self.step("tensor", eP, lambda t, cp=cp, c=c, mat=mat: t.matmul(
                        self.PS_S[:, 1024 + cp:1025 + cp], mat[:, :],
                        sucol[:, c:c + 1],
                        start=(c == 0), stop=(c == DCH - 1)))
            self.wait("vector", eP, vhs)
            vhs2 = self.step("vector", eV, lambda v: v.tensor_copy(
                HSU[:, :], self.PS_S[:, 1024:1024 + DCH]))
            self.ps_release(["S2"], eV, vhs2)

            ao_ch = [0] * NJ
            for j2 in range(NJ):
                ao_ch[j2] = self.dma(
                    "gpsimd", AOR[j2][:, :].rearrange("p (c t) -> p c t", t=512),
                    self.ao_dram[:, :, j2 * 512:(j2 + 1) * 512].rearrange("c p t -> p c t"),
                    dG)
            xow_used = {}
            xmf_used = {}
            wo_used = {}
            xn_used = {}
            xb_used = {}
            sqt_used = {}
            self.x1_half = [0] * NJ
            x1v = X1[:, :].rearrange("p (c q) -> p c q", q=Q)
            for j2 in range(NJ):
                self.wait("tensor", dG, ao_ch[j2])
                x1j = x1v[:, :, j2 * 512:(j2 + 1) * 512]
                for oc in range(DCH):
                    bi = oc % 2
                    key = j2 * DCH + oc
                    if key >= 2:
                        self.wait("gpsimd", eP, wo_used[key - 2])
                    wod = self.dma("gpsimd",
                                   WOB[bi][:, :].rearrange("p (c o) -> p c o", o=128),
                                   self.wo[oc, :, :, :].rearrange("c p o -> p c o"), dG)
                    if key - 2 in xow_used:
                        self.wait("sync", eV, xow_used[key - 2])
                    xd = self.dma("sync", XOW[bi][:, :],
                                  self.xt[oc * 128:(oc + 1) * 128, j2 * 512:(j2 + 1) * 512],
                                  dS)
                    self.wait("tensor", dG, wod)
                    sl = oc % 2
                    self.ps_guard("tensor", ["S%d" % sl])
                    vl = None
                    for dc in range(DCH):
                        vl = self.step("tensor", eP, lambda t, dc=dc, bi=bi, sl=sl, j2=j2:
                                       t.matmul(
                            self.PS_S[:, sl * 512:(sl + 1) * 512],
                            WOB[bi][:, dc * 128:(dc + 1) * 128],
                            AOR[j2][:, dc * 512:(dc + 1) * 512],
                            start=(dc == 0), stop=(dc == DCH - 1)))
                    wo_used[key] = eP.n
                    self.wait("vector", eP, vl)
                    self.wait("vector", dS, xd)
                    if key - 2 in xmf_used:
                        self.wait("vector", dS, xmf_used[key - 2][0])
                        self.wait("vector", eA, xmf_used[key - 2][1])
                    vmf = self.step("vector", eV, lambda v, sl=sl, bi=bi: v.tensor_add(
                        XMF[bi][:, :], self.PS_S[:, sl * 512:(sl + 1) * 512],
                        XOW[bi][:, :]))
                    self.ps_release(["S%d" % sl], eV, vmf)
                    xow_used[key] = vmf
                    self.wait("sync", eV, vmf)
                    std = self.dma("sync",
                                   self.xmid_dram[oc, :, j2 * 512:(j2 + 1) * 512],
                                   XMF[bi][:, :], dS)
                    self.wait("scalar", eV, vmf)
                    if key - 2 in xb_used:
                        self.wait("scalar", eP, xb_used[key - 2][0])
                        self.wait("scalar", eV, xb_used[key - 2][1])
                    vxb = self.step("scalar", eA, lambda s, bi=bi: s.activation(
                        XB[bi][:, :], XMF[bi][:, :], AF.Copy, bias=0.0, scale=1.0))
                    xmf_used[key] = (std, vxb)
                    if key - 2 in sqt_used:
                        self.wait("vector", eP, sqt_used[key - 2])
                    vsq = self.step("vector", eV, lambda v, bi=bi: v.tensor_mul(
                        SQT[bi][:, :], XMF[bi][:, :], XMF[bi][:, :]))
                    self.wait("tensor", eA, vxb)
                    if oc == 0:
                        self.ps_guard("tensor", ["S4", "S5"])
                    vsx = self.step("tensor", eP, lambda t, oc=oc, bi=bi: t.matmul(
                        self.PS_S[0:1, 2048:2560], self.ones_c[:, :], XB[bi][:, :],
                        start=(oc == 0), stop=(oc == DCH - 1)))
                    self.wait("tensor", eV, vsq)
                    v2 = self.step("tensor", eP, lambda t, oc=oc, bi=bi: t.matmul(
                        self.PS_S[0:1, 2560:3072], self.ones_c[:, :], SQT[bi][:, :],
                        start=(oc == 0), stop=(oc == DCH - 1)))
                    sqt_used[key] = v2
                    if bi in xn_used:
                        self.wait("vector", eP, xn_used[bi])
                    self.wait("vector", eA, vxb)
                    vxn = self.step("vector", eV, lambda v, bi=bi, oc=oc: v.tensor_scalar(
                        XN[bi][:, :], XB[bi][:, :], sucolf[:, oc:oc + 1], None,
                        op0=ALU.mult))
                    xb_used[key] = (vsx, vxn)
                    sl2 = 2 + oc % 2
                    self.ps_guard("tensor", ["S%d" % sl2])
                    self.wait("tensor", eV, vxn)
                    vlo = self.step("tensor", eP, lambda t, sl2=sl2, bi=bi: t.matmul(
                        self.PS_S[:, sl2 * 512:(sl2 + 1) * 512], self.hmat[:, :],
                        XN[bi][:, :], start=True, stop=True))
                    xn_used[bi] = vlo
                    self.wait("vector", eP, vlo)
                    vx1 = self.step("vector", eV, lambda v, sl2=sl2, oc=oc, j2=j2:
                                    v.tensor_copy(
                        X1[:, oc * Q + j2 * 512:oc * Q + (j2 + 1) * 512],
                        self.PS_S[:, sl2 * 512:(sl2 + 1) * 512]))
                    self.ps_release(["S%d" % sl2], eV, vx1)
                    self.butterfly_triggers(x1j, oc + 1, DCH, QCB, BFA, BFB, 128, 512)
                self.wait("vector", eP, v2)
                a2 = self.rows2[:, j2 * 512:(j2 + 1) * 512]
                b2 = self.rows2[:, Q + j2 * 512:Q + (j2 + 1) * 512]
                self.quiet("vector", lambda v, a2=a2: v.tensor_scalar_mul(
                    a2, self.PS_S[0:1, 2048:2560], 1.0 / DM))
                self.quiet("vector", lambda v, b2=b2: v.tensor_scalar_mul(
                    b2, self.PS_S[0:1, 2560:3072], 1.0 / DM))
                self.quiet("vector", lambda v, a2=a2: v.tensor_tensor(
                    t2s[:, :], a2, a2, ALU.mult))
                vv = self.step("vector", eV, lambda v, b2=b2: v.tensor_sub(
                    b2, b2, t2s[:, :]))
                self.ps_release(["S4", "S5"], eV, vv)
                self.wait("scalar", eV, vv)
                self.quiet("scalar", lambda s, b2=b2: s.activation(
                    b2, b2, AF.Ln, bias=self.epsc[0:1, 0:1], scale=1.0))
                va2 = self.step("scalar", eA, lambda s, b2=b2: s.activation(
                    b2, b2, AF.Exp, bias=0.0, scale=-0.5))
                self.wait("vector", eA, va2)
                vb2 = self.step("vector", eV, lambda v, a2=a2, b2=b2: v.scalar_tensor_tensor(
                    t2s[:, :], a2, -1.0, b2, op0=ALU.mult, op1=ALU.mult))
                self.wait("tensor", eV, vb2)
                pa = self.bcast_row(b2, self.PS_S[:, 2048:2560], ["S4"])
                pb = self.bcast_row(t2s[0:1, :], self.PS_S[:, 2560:3072], ["S5"])
                self.wait("vector", eP, pb)
                va2b = self.step("vector", eV, lambda v: v.tensor_copy(
                    A2B[:, :], self.PS_S[:, 2048:2560]))
                vb2b = self.step("vector", eV, lambda v: v.tensor_copy(
                    B2B[:, :], self.PS_S[:, 2560:3072]))
                self.ps_release(["S4", "S5"], eV, vb2b)
                for oc in range(DCH):
                    self.quiet("vector", lambda v, oc=oc, j2=j2: v.tensor_tensor(
                        t2f[:, :], X1[:, oc * Q + j2 * 512:oc * Q + (j2 + 1) * 512],
                        A2B[:, :], ALU.mult))
                    self.step("vector", eV, lambda v, oc=oc, j2=j2: v.scalar_tensor_tensor(
                        X1[:, oc * Q + j2 * 512:oc * Q + (j2 + 1) * 512],
                        B2B[:, :], HSU[:, oc:oc + 1], t2f[:, :],
                        op0=ALU.mult, op1=ALU.add))
                self.x1_half[j2] = self.eV.n
            self.x1_ready = self.eV.n
            self.wo_tail = {"eP": eP.n, "eA": self.eA.n, "dS": dS.n, "dG": dG.n,
                            "eV": self.eV.n}
            if cfg.debug2:
                self.wait("sync", eV)
                self.dma("sync", self.hsu_out, HSU[:, :], dS)

    def phase_ffn(self):
        cfg = self.cfg
        DM, IN, Q, DCH, ICH, NJ = cfg.DM, cfg.IN, cfg.Q, cfg.DCH, cfg.ICH, cfg.NJ
        eP, eA, eV, eG, dG, dS = self.eP, self.eA, self.eV, self.eG, self.dG, self.dS
        X1 = self.X1

        HB = self.HB
        X1 = self.X1
        HB2 = self.sbuf("HB2", [128, IN // 128 * 512], BF16)

        def emit_butterflies(hbuf, lo_chunks, hi_chunks, BFA, BFB, tq, ts_list=None):
            self.emit_butterflies(
                hbuf[:, :].rearrange("p (c q) -> p c q", q=512),
                lo_chunks, hi_chunks, BFA, BFB, tq, 512, ts_list=ts_list)

        def emit_cross_butterfly(hbuf, dist, block_lo, BFS, tq, ts_list=None):
            self.emit_cross_butterfly(
                hbuf[:, :].rearrange("p (c q) -> p c q", q=512),
                dist, block_lo, BFS, tq, 512, BFS2=BFB, ts_list=ts_list)

        c2a = contextlib.ExitStack()
        with c2a:
            sb = lambda n_, s_, d_: self.sbuf(n_, s_, d_, c2a)
            UPS = [[sb("UPS%d%d" % (h_, p_), [128, 512], BF16) for p_ in range(3)]
                   for h_ in range(NJ)]
            GTS = [[sb("GTS%d%d" % (h_, p_), [128, 512], BF16) for p_ in range(3)]
                   for h_ in range(NJ)]
            QC = max(ICH // 2, 2)
            hb_ready = [0, 0]
            BFA = sb("BFA2", [128, max(ICH // 2, QC) * 128], BF16)
            BFB = sb("BFB2", [128, max(ICH // 2, QC) * 128], BF16)
            WUGB = [sb("WUGB0", [128, 2 * DCH * 128], BF16),
                    sb("WUGB1", [128, 2 * DCH * 128], BF16)]
            self.wait("scalar", eV, self.wo_tail["eV"])
            self.wait("vector", eP, self.wo_tail["eP"])
            self.wait("vector", eA, self.wo_tail["eA"])
            self.wait("vector", dS, self.wo_tail["dS"])
            self.wait("vector", dG, self.wo_tail["dG"])
            if cfg.debug:
                self.step("vector", self.eV, lambda v: v.memset(HB2[:, :], 0.0))
                wclaim = 0
                for tns in (WUGB[0], WUGB[1]):
                    wclaim = self.step("vector", self.eV,
                                       lambda v, tns=tns: v.memset(tns[:, :], 0.0))
                self.wait("gpsimd", eV, wclaim)
            upg_pe = {}
            ups_used = {}
            bfq = [[], []]
            for ocg in range(ICH):
                bi = ocg % 2
                if ocg >= 2:
                    self.wait("gpsimd", eP, upg_pe[ocg - 2])
                wud = self.dma("gpsimd",
                               WUGB[bi][:, :].rearrange("p (w c o) -> p w c o", w=2, o=128),
                               self.wug[ocg, :, :, :, :].rearrange("w c p o -> p w c o"),
                               dG)
                self.wait("tensor", dG, wud)
                for hf in range(NJ):
                    hbuf = HB if hf == 0 else HB2
                    st = 2 * ocg + hf
                    pu = ocg % 3
                    slu = st % 2
                    slg = 2 + st % 2
                    self.wait("tensor", eV, self.x1_half[hf])
                    self.ps_guard("tensor", ["S%d" % slu])
                    vu = None
                    for dc in range(DCH):
                        vu = self.step("tensor", eP, lambda t, dc=dc, slu=slu, bi=bi, hf=hf:
                                       t.matmul(
                            self.PS_S[:, slu * 512:(slu + 1) * 512],
                            WUGB[bi][:, dc * 128:(dc + 1) * 128],
                            X1[:, dc * Q + hf * 512:dc * Q + (hf + 1) * 512],
                            start=(dc == 0), stop=(dc == DCH - 1)))
                    self.ps_guard("tensor", ["S%d" % slg])
                    vg = None
                    for dc in range(DCH):
                        vg = self.step("tensor", eP, lambda t, dc=dc, slg=slg, bi=bi, hf=hf:
                                       t.matmul(
                            self.PS_S[:, slg * 512:(slg + 1) * 512],
                            WUGB[bi][:, DCH * 128 + dc * 128:DCH * 128 + (dc + 1) * 128],
                            X1[:, dc * Q + hf * 512:dc * Q + (hf + 1) * 512],
                            start=(dc == 0), stop=(dc == DCH - 1)))
                    self.wait("scalar", eP, vu)
                    if (hf, pu) in ups_used:
                        self.wait("scalar", eV, ups_used[(hf, pu)])
                    vue = self.step("scalar", eA, lambda s, slu=slu, ocg=ocg, hf=hf, pu=pu:
                                    s.activation(
                        UPS[hf][pu][:, :], self.PS_S[:, slu * 512:(slu + 1) * 512],
                        AF.Copy, bias=0.0, scale=self.upsc[:, ocg:ocg + 1]))
                    self.ps_release(["S%d" % slu], eA, vue)
                    self.wait("scalar", eP, vg)
                    vge = self.step("scalar", eA, lambda s, slg=slg, ocg=ocg, hf=hf, pu=pu:
                                    s.activation(
                        GTS[hf][pu][:, :], self.PS_S[:, slg * 512:(slg + 1) * 512],
                        AF.Silu, bias=0.0, scale=self.gtsc[:, ocg:ocg + 1]))
                    self.ps_release(["S%d" % slg], eA, vge)
                    self.wait("vector", eA, vge)
                    vh = self.step("vector", eV, lambda v, ocg=ocg, hbuf=hbuf, hf=hf, pu=pu:
                                   v.tensor_mul(
                        hbuf[:, ocg * 512:(ocg + 1) * 512], GTS[hf][pu][:, :],
                        UPS[hf][pu][:, :]))
                    ups_used[(hf, pu)] = vh
                upg_pe[ocg] = eP.n
                NTS = 4
                for hb_i, hbuf in enumerate((HB, HB2)):
                    if (ocg + 1) % QC == 0:
                        qlo = ocg + 1 - QC
                        for ts in range(NTS):
                            bfq[hb_i].append(
                                lambda ts=ts, qlo=qlo, hbuf=hbuf: emit_butterflies(
                                    hbuf, qlo, qlo + QC, BFA, BFB, 128, ts_list=[ts]))
                    d = 2 * QC
                    while d < ICH and (ocg + 1) % d == 0:
                        for ts in range(NTS):
                            bfq[hb_i].append(
                                lambda ts=ts, d=d, blo=ocg + 1 - d, hbuf=hbuf:
                                emit_cross_butterfly(hbuf, d // 2, blo, BFA, 128,
                                                     ts_list=[ts]))
                        d *= 2
                for hb_i in range(2):
                    nemit = len(bfq[hb_i]) if ocg == ICH - 1 else min(2, len(bfq[hb_i]))
                    for _ in range(nemit):
                        bfq[hb_i].pop(0)()
                    if ocg == ICH - 1:
                        hb_ready[hb_i] = self.eV.n
            c2a_eA = self.eA.n
            c2a_dG = dG.n
            c2a_eP = eP.n

        c2b = contextlib.ExitStack()
        with c2b:
            sb = lambda n_, s_, d_: self.sbuf(n_, s_, d_, c2b)
            WDB = [sb("WDB0", [128, ICH * 128], BF16), sb("WDB1", [128, ICH * 128], BF16)]
            t3p = [sb("t3c2a", [128, 512], BF16), sb("t3c2b", [128, 512], BF16)]
            XMR = [sb("XMR0", [128, 512], F32), sb("XMR1", [128, 512], F32)]
            OUTS = [sb("OUTS0", [128, 512], F32), sb("OUTS1", [128, 512], F32)]

            if cfg.debug2:
                self.wait("sync", eV, hb_ready[0])
                self.dma("sync", self.hb_out, HB[:, :], dS)
                self.dma("sync", self.x1_out, X1[:, :], dS)
                self.wait("tensor", dS)
            claim = hb_ready[NJ - 1]
            if cfg.debug:
                for tns in (WDB[0], WDB[1], t3p[0], t3p[1], XMR[0], XMR[1],
                            OUTS[0], OUTS[1]):
                    claim = self.step("vector", eV,
                                      lambda v, tns=tns: v.memset(tns[:, :], 0.0))
            self.wait("gpsimd", eA, c2a_eA)
            self.wait("gpsimd", eV, claim)
            self.wait("gpsimd", eP, c2a_eP)
            self.wait("sync", eV, claim)
            self.wait("sync", eP, c2a_eP)
            self.wait("sync", dG, c2a_dG)
            lo_eA = {}
            H2 = ICH // 2
            for hf in range(NJ):
                hbuf = HB if hf == 0 else HB2
                self.wait("tensor", eV, hb_ready[hf])
                for c in range(H2):
                    sl = (c % 2) * 2
                    self.ps_guard("tensor", ["S%d" % sl, "S%d" % (sl + 1)])
                    for mat, off in ((self.hmat, 0), (self.hneg, 512)):
                        self.quiet("tensor", lambda t, c=c, sl=sl, hbuf=hbuf, off=off:
                                   t.matmul(
                            self.PS_S[:, sl * 512 + off:sl * 512 + off + 512],
                            self.hmat[:, :], hbuf[:, c * 512:(c + 1) * 512],
                            start=True, stop=False))
                    vmm = None
                    for mat, off in ((self.hmat, 0), (self.hneg, 512)):
                        vmm = self.step("tensor", eP, lambda t, c=c, sl=sl, hbuf=hbuf,
                                        mat=mat, off=off: t.matmul(
                            self.PS_S[:, sl * 512 + off:sl * 512 + off + 512],
                            mat[:, :], hbuf[:, (c + H2) * 512:(c + H2 + 1) * 512],
                            start=False, stop=True))
                    self.wait("scalar", eP, vmm)
                    self.quiet("scalar", lambda s, c=c, sl=sl, hbuf=hbuf: s.activation(
                        hbuf[:, c * 512:(c + 1) * 512],
                        self.PS_S[:, sl * 512:sl * 512 + 512],
                        AF.Copy, bias=0.0, scale=1.0))
                    lo_eA[hf] = self.step("scalar", eA, lambda s, c=c, sl=sl, hbuf=hbuf:
                                          s.activation(
                        hbuf[:, (c + H2) * 512:(c + H2 + 1) * 512],
                        self.PS_S[:, sl * 512 + 512:sl * 512 + 1024],
                        AF.Copy, bias=0.0, scale=1.0))
                    self.ps_release(["S%d" % sl, "S%d" % (sl + 1)], eA, lo_eA[hf])
            self.wait("scalar", eV, hb_ready[NJ - 1])
            dn_pe = {}
            outs_used = {}
            xmr_used = {}
            for oc in range(DCH):
                bi = oc % 2
                if oc >= 2:
                    self.wait("gpsimd", eP, dn_pe[oc - 2])
                wdd = self.dma("gpsimd",
                               WDB[bi][:, :].rearrange("p (c o) -> p c o", o=128),
                               self.wdn[oc, :, :, :].rearrange("c p o -> p c o"), dG)
                self.wait("tensor", dG, wdd)
                for hf in range(NJ):
                    hbuf = HB if hf == 0 else HB2
                    if (oc - 1, hf) in xmr_used:
                        self.wait("sync", eV, xmr_used[(oc - 1, hf)])
                    xmd = self.dma("sync", XMR[hf % 2][:, :],
                                   self.xmid_dram[oc, :, hf * 512:(hf + 1) * 512], dS)
                    self.wait("tensor", eA, lo_eA.get(hf, 0))
                    sl = 4 + hf % 2
                    self.ps_guard("tensor", ["S%d" % sl])
                    vd = None
                    for fc in range(ICH):
                        vd = self.step("tensor", eP, lambda t, fc=fc, sl=sl, bi=bi, hbuf=hbuf:
                                       t.matmul(
                            self.PS_S[:, sl * 512:(sl + 1) * 512],
                            WDB[bi][:, fc * 128:(fc + 1) * 128],
                            hbuf[:, fc * 512:(fc + 1) * 512],
                            start=(fc == 0), stop=(fc == ICH - 1)))
                    self.wait("scalar", eP, vd)
                    if (oc - 1, hf) in xmr_used:
                        self.wait("scalar", eV, xmr_used[(oc - 1, hf)])
                    vde = self.step("scalar", eA, lambda s, sl=sl, oc=oc, hf=hf: s.activation(
                        t3p[hf % 2][:, :], self.PS_S[:, sl * 512:(sl + 1) * 512],
                        AF.Copy, bias=0.0, scale=self.dnsc[:, oc:oc + 1]))
                    self.ps_release(["S%d" % sl], eA, vde)
                    self.wait("vector", eA, vde)
                    self.wait("vector", dS, xmd)
                    if (oc - 1, hf) in outs_used:
                        self.wait("vector", dS, outs_used[(oc - 1, hf)])
                    vout = self.step("vector", eV, lambda v, hf=hf: v.tensor_add(
                        OUTS[hf % 2][:, :], t3p[hf % 2][:, :], XMR[hf % 2][:, :]))
                    xmr_used[(oc, hf)] = vout
                    self.wait("sync", eV, vout)
                    outs_used[(oc, hf)] = self.dma(
                        "sync", self.outy[oc * 128:(oc + 1) * 128, hf * 512:(hf + 1) * 512],
                        OUTS[hf % 2][:, :], dS)
                dn_pe[oc] = eP.n

    def emit(self):
        with self.nc.Block() as block:
            @block.sync
            def _(e):
                for fn in self.q["sync"]:
                    fn(e)

            @block.gpsimd
            def _(e):
                for fn in self.q["gpsimd"]:
                    fn(e)

            @block.tensor
            def _(e):
                for fn in self.q["tensor"]:
                    fn(e)

            @block.scalar
            def _(e):
                for fn in self.q["scalar"]:
                    fn(e)

            @block.vector
            def _(e):
                for fn in self.q["vector"]:
                    fn(e)



def pack_inputs(cfg, x, Wq, Wk, Wv, Wo, SU_ug, SU_dp, Wscale_ug, Wscale_dp,
                W_up, W_gate, W_down):
    DM, TT, IN = cfg.DM, cfg.TT, cfg.IN
    DCH, ICH, Q = cfg.DCH, cfg.ICH, cfg.Q
    wo_p = np.ascontiguousarray(
        Wo.reshape(DCH, 128, DCH, 128).transpose(2, 0, 1, 3))
    wup_q = W_up.T.reshape(DCH, 128, ICH, 128).transpose(2, 0, 1, 3)
    wgt_q = W_gate.T.reshape(DCH, 128, ICH, 128).transpose(2, 0, 1, 3)
    wug_p = np.ascontiguousarray(np.stack([wup_q, wgt_q], axis=1))
    wdn_p = np.ascontiguousarray(
        W_down.T.reshape(ICH, 128, DCH, 128).transpose(2, 0, 1, 3))
    su_p = np.ascontiguousarray(SU_ug.reshape(DCH, 128).T)
    upsc = np.ascontiguousarray(
        (Wscale_ug[:IN] * 64.0 * cfg.s1 * SU_dp).reshape(ICH, 128).T)
    gtsc = np.ascontiguousarray(
        (Wscale_ug[IN:] * 64.0 * cfg.s1).reshape(ICH, 128).T)
    dnsc = np.ascontiguousarray(
        (Wscale_dp * 64.0 * cfg.s2).reshape(DCH, 128).T)
    wq_c = np.ascontiguousarray(Wq)
    wk_c = np.ascontiguousarray(Wk)
    wv_c = np.ascontiguousarray(Wv)

    in_maps = []
    for core in range(8):
        b, qid = core // 4, core % 4
        perm = np.concatenate([
            np.arange(qid * Q, (qid + 1) * Q),
            np.arange(0, qid * Q),
            np.arange((qid + 1) * Q, TT)])
        xt_c = np.ascontiguousarray(x[b][perm].T)
        kval = np.zeros((1, TT), np.float32)
        kval[0, :Q + qid * Q] = 1.0
        in_maps.append(dict(
            xt=xt_c, wq=wq_c, wk=wk_c, wv=wv_c, wo=wo_p,
            wug=wug_p, wdn=wdn_p, suug=su_p, kvalid=kval,
            upsc_in=upsc, gtsc_in=gtsc, dnsc_in=dnsc))
    return in_maps


_PROGRAM_CACHE = {}


def get_program(cfg):
    key = (cfg.DM, cfg.TT, cfg.IN)
    if key not in _PROGRAM_CACHE:
        _PROGRAM_CACHE[key] = B(cfg).build()
    return _PROGRAM_CACHE[key]


def kernel(L0_attn_x, L0_attn_Wq, L0_attn_Wk, L0_attn_Wv, L0_attn_Wo,
           L0_ffn_SU_ug, L0_ffn_SU_dp, L0_ffn_Wscale_ug, L0_ffn_Wscale_dp,
           L0_ffn_W_up, L0_ffn_W_gate, L0_ffn_W_down):
    cfg = Cfg(DM=2048, TT=4096, IN=8192)
    x = np.asarray(L0_attn_x, np.float32)
    in_maps = pack_inputs(
        cfg, x, np.asarray(L0_attn_Wq, np.float32), np.asarray(L0_attn_Wk, np.float32),
        np.asarray(L0_attn_Wv, np.float32), np.asarray(L0_attn_Wo, np.float32),
        np.asarray(L0_ffn_SU_ug, np.float32), np.asarray(L0_ffn_SU_dp, np.float32),
        np.asarray(L0_ffn_Wscale_ug, np.float32), np.asarray(L0_ffn_Wscale_dp, np.float32),
        np.asarray(L0_ffn_W_up, np.float32), np.asarray(L0_ffn_W_gate, np.float32),
        np.asarray(L0_ffn_W_down, np.float32))
    nc = get_program(cfg)
    res = run_bass_kernel_spmd(nc, in_maps, core_ids=list(range(8)))
    out = np.empty((2, cfg.TT, cfg.DM), np.float32)
    for core in range(8):
        b, qid = core // 4, core % 4
        out[b, qid * cfg.Q:(qid + 1) * cfg.Q, :] = res.results[core]["outy"].T
    return out
